# version 53
# baseline (speedup 1.0000x reference)
import numpy as np
import ml_dtypes
import contextlib
import concourse.bass as bass
import concourse.mybir as mybir
from concourse.bass_utils import run_bass_kernel_spmd

F32 = mybir.dt.float32
BF16 = mybir.dt.bfloat16
I32 = mybir.dt.int32
AF = mybir.ActivationFunctionType
ALU = mybir.AluOpType
AX = mybir.AxisListType
f32 = np.float32

NB = 2
S = 2048
DM = 1024
H = 16
FF = 4096
NG = 4
KT = 16
PAIRS = 8
EPS = 1e-5

ENGINES = ("sync", "gpsimd", "tensor", "vector", "scalar")


class Plan:

    def __init__(self):
        self.ops = {e: [] for e in ENGINES}
        self.sem = {}

    def val(self, s):
        return self.sem.get(s, 0)

    def w(self, eng, s, v):
        if v > 0:
            self.ops[eng].append(("w", s, v))

    def d(self, eng):
        self.ops[eng].append(("d",))

    def o(self, eng, method, *args, **kw):
        rec = ["o", method, args, kw, None]
        self.ops[eng].append(rec)
        return rec

    def inc(self, rec, s, n=1):
        assert rec[4] is None
        self.sem[s] = self.sem.get(s, 0) + n
        rec[4] = (s, n)
        return self.sem[s]

    def oi(self, eng, method, *args, _s=None, _n=1, **kw):
        rec = self.o(eng, method, *args, **kw)
        self.inc(rec, _s, _n)
        return self.sem[_s]

    def hinc(self, eng, s):
        self.d(eng)
        m = "engine_nop" if eng in ("vector", "gpsimd") else "nop"
        return self.oi(eng, m, _s=s)

    BARRIER_EXCLUDE = ("sSC0", "sSC1", "sDQ0", "sDQ1", "sG")

    def barrier(self):
        snap = dict(self.sem)
        for e in ENGINES:
            for s, v in snap.items():
                if v > 0 and s not in self.BARRIER_EXCLUDE:
                    self.ops[e].append(("w", s, v))


def build_program(debug=False):
    nc = bass.Bass("TRN2", target_bir_lowering=False)

    d_x = nc.dram_tensor("x", [NB, S, DM], F32, kind="ExternalInput").ap()
    d_xT = nc.dram_tensor("xt", [DM, NB * S], F32, kind="ExternalInput").ap()
    d_eye = nc.dram_tensor("eye", [128, 128], BF16, kind="ExternalInput").ap()
    d_wqbf = nc.dram_tensor("wqbf", [1024, 1536], BF16, kind="Internal").ap()
    d_wqkv = nc.dram_tensor("wqkv", [DM, 3 * DM], F32, kind="ExternalInput").ap()
    d_wout = nc.dram_tensor("wout", [DM, DM], F32, kind="ExternalInput").ap()
    d_bqkv = nc.dram_tensor("bqkv_c", [128, 24], F32, kind="ExternalInput").ap()
    d_bout = nc.dram_tensor("bout_r", [1, DM], F32, kind="ExternalInput").ap()
    d_bv = nc.dram_tensor("bv_r", [1, DM], F32, kind="ExternalInput").ap()
    d_gam = nc.dram_tensor("gam_bc", [128, DM], F32, kind="ExternalInput").ap()
    d_bet = nc.dram_tensor("bet_bc", [128, DM], F32, kind="ExternalInput").ap()
    d_q1 = nc.dram_tensor("q1", [DM, FF], I32, kind="ExternalInput").ap()
    d_s1 = nc.dram_tensor("s1bc", [DM, FF], F32, kind="ExternalInput").ap()
    d_b1 = nc.dram_tensor("b1_c", [128, 32], F32, kind="ExternalInput").ap()
    d_q2 = nc.dram_tensor("q2", [FF, DM], I32, kind="ExternalInput").ap()
    d_s2 = nc.dram_tensor("s2bc", [FF, DM], F32, kind="ExternalInput").ap()
    d_b2 = nc.dram_tensor("b2_r", [1, DM], F32, kind="ExternalInput").ap()
    d_pe2 = nc.dram_tensor("pe2", [128, S], F32, kind="ExternalInput").ap()
    d_y = nc.dram_tensor("y", [NB, S, DM], F32, kind="ExternalOutput").ap()
    d_w1s = nc.dram_tensor("w1s", [DM, FF], BF16, kind="Internal").ap()
    d_w2s = nc.dram_tensor("w2s", [FF, DM], BF16, kind="Internal").ap()
    d_xbfT = nc.dram_tensor("xbft", [DM, NB * S], BF16, kind="Internal").ap()

    x_rows = d_x.rearrange("b s d -> (b s) d")
    y_rows = d_y.rearrange("b s d -> (b s) d")

    dbg_t = {}
    if debug:
        for nm, shp, dt in (("dbg_q", [128, 8 * S], BF16), ("dbg_k", [128, 8 * S], BF16),
                            ("dbg_v", [128, KT * H * 80], BF16), ("dbg_cneg", [128, 1], F32),
                            ("dbg_O", [128, 8 * S], BF16), ("dbg_x1", [128, 4 * DM], F32),
                            ("dbg_h", [128, 8 * 512], BF16), ("dbg_y", [128, 32 * 512], BF16),
                            ("dbg2_q", [128, 8 * S], BF16), ("dbg2_k", [128, 8 * S], BF16),
                            ("dbg2_v", [128, KT * H * 80], BF16), ("dbg2_cneg", [128, 1], F32),
                            ("dbg2_O", [128, 8 * S], BF16), ("dbg2_x1", [128, 4 * DM], F32),
                            ("dbg2_h", [128, 8 * 512], BF16), ("dbg2_y", [128, 32 * 512], BF16)):
            dbg_t[nm] = nc.dram_tensor(nm, shp, dt, kind="ExternalOutput").ap()

    SB_BASE = 16512
    SBUF_LIMIT = 229344
    alloc = nc.alloc_sbuf_tensor_at
    cur = [SB_BASE]

    def place(name, shape, dt, at=None):
        if at is None:
            at = cur[0]
        nbytes = int(np.prod(shape[1:])) * mybir.dt.size(dt)
        t = alloc(name, shape, dt, offset=at)
        end = (at + nbytes + 31) // 32 * 32
        cur[0] = max(cur[0], end)
        assert cur[0] <= SBUF_LIMIT, (name, cur[0])
        return t

    ones_bf = place("ones_bf", [128, 128], BF16)
    ones_f = place("ones_f", [128, 64], F32)
    bqkv_c = place("bqkv_c", [128, 24], F32)
    b1_c = place("b1_c", [128, 32], F32)
    cneg = place("cneg", [128, 1], F32)
    nmax = place("nmax", [1, 72], F32)
    nmaxbf = place("nmaxbf", [1, 8], BF16)
    stats = place("stats", [128, 12], F32)
    stats2 = place("stats2", [128, 12], F32)
    stats3 = place("stats3", [128, 12], F32)
    mvp = place("mvp", [128, 24], F32)
    mvp2 = place("mvp2", [128, 4], F32)
    brow_o = place("brow_o", [128, 1024], BF16)
    brow_2 = place("brow_2", [128, 1024], BF16)
    brow_v = place("brow_v", [128, 1024], BF16)
    w1buf_x = place("w1buf2", [128, 1024], BF16)
    gam_bc = place("gam_bc", [128, DM], F32)
    bet_bc = place("bet_bc", [128, DM], F32)
    wout_bf = place("wout_bf", [128, 8 * DM], BF16)
    pe2 = place("pe2", [128, S], F32)
    assert cur[0] <= SB_BASE + 43008, cur[0]

    B0 = SB_BASE + 43008
    q_fm = place("q_fm", [128, 8 * S], BF16, at=B0)
    k_fm = place("k_fm", [128, 8 * S], BF16, at=B0 + 32768)
    v_tm = place("v_tm", [128, KT * H * 80], BF16, at=B0 + 65536)
    B_END = B0 + 65536 + 40960
    x_tm2 = [place(f"x_tm2_{i}", [128, DM], F32, at=B0 + i * 4096) for i in range(2)]
    u_sb = place("u_sb", [128, DM], F32, at=B0 + 8192)
    x1buf = [place(f"x1_{i}", [128, 4 * DM], F32, at=B0 + 12288 + i * 16384)
             for i in range(2)]
    t2_sb = place("t2_sb", [128, DM], F32, at=B0 + 45056)
    h_tm = [place(f"h_tm{i}", [128, DM], BF16, at=B0 + 49152 + i * 2048)
            for i in range(4)]
    h_fm = place("h_fm", [128, 8 * 512], BF16, at=B0 + 57344)
    y_g = place("y_g", [128, 32 * 512], BF16, at=B0 + 65536)
    w1buf = [place(f"w1buf{i}", [128, 1024], BF16, at=B0 + 98304 + i * 2048)
             for i in range(2)]
    w2buf = [place(f"w2buf{i}", [128, 1024], BF16, at=B0 + 102400 + i * 2048)
             for i in range(2)]
    assert B0 + 102400 + 4096 <= B_END

    C0 = B_END
    wqh = place("wqh", [128, 8 * 1536], BF16, at=C0)
    x_fm = [place(f"x_fm{i}", [128, 8 * 512], BF16, at=C0 + 24576 + i * 8192)
            for i in range(2)]
    sq_sb = [place(f"sq_sb{i}", [128, S], BF16, at=C0 + 24576 + i * 8192)
             for i in range(2)]
    O_all = place("O_all", [128, 8 * S], BF16, at=C0)
    pT = [place(f"pT{i}", [128, 1024], BF16, at=C0 + 32768 + i * 2048) for i in range(2)]
    recb = place("recb", [128, 1024], F32, at=C0 + 36864)
    bc_sb = place("bc_sb", [128, 1024], F32, at=C0 + 40960)
    C_END = C0 + 45056
    out_sb = [place(f"out_sb{i}", [128, DM], F32, at=C0 + 36864 + i * 4096)
              for i in range(2)]

    D0 = C_END
    dqq = [place(f"dqq{i}", [128, 512], I32, at=D0 + i * 2048) for i in range(2)]
    dqs = [place(f"dqs{i}", [128, 512], F32, at=D0 + 4096 + i * 2048) for i in range(2)]
    dqt = [place(f"dqt{i}", [128, 512], F32, at=D0 + 8192 + i * 2048) for i in range(2)]
    dqw = [place(f"dqw{i}", [128, 512], BF16, at=D0 + 12288 + i * 1024) for i in range(2)]
    brow_f = place("brow_f", [128, 1024], F32, at=C0 + 36864)
    w2buf_x = place("w2buf2", [128, 1024], BF16, at=D0 + 14336)
    eye_sb = place("eye_sb", [128, 128], BF16, at=D0 + 16384)
    assert D0 + 16640 <= SBUF_LIMIT, D0 + 16640

    PS_A = nc.alloc_psum_tensor("PS_A", [128, 2048], F32)
    PS_B = nc.alloc_psum_tensor("PS_B", [128, 2048], F32)

    es = contextlib.ExitStack()
    sems = {}
    for s in ("sXC0", "sXC1", "sWQ", "sWO", "sXT0", "sXT1",
              "sW1a", "sW1b", "sW1c", "sW2a", "sW2b", "sW2c",
              "sDQ0", "sDQ1", "sSC0", "sSC1", "sOUT0", "sOUT1", "sMS",
              "sTPx0", "sTPx1", "sTPv0", "sTPv1", "sTPh", "sT", "sV", "sS", "sG",
              "sWQ2"):
        sems[s] = es.enter_context(nc.semaphore(s))

    P = Plan()

    def dbg_dump(pairs):
        if not debug:
            return
        P.barrier()
        for nm, ap in pairs:
            P.oi("sync", "dma_start", out=dbg_t[nm], in_=ap, _s="sOUT0", _n=16)
        P.barrier()

    psA_free = {0: ("sV", 0), 1: ("sV", 0)}
    psB_half = {0: ("sV", 0), 1: ("sV", 0)}
    psBq_free = [("sV", 0)] * 4
    vtmp_cons = {0: 0, 1: 0}
    xfm_cons = {0: 0, 1: 0}
    xfm_ready = {0: 0, 1: 0}
    psA_vfree = [("sV", 0)] * 4
    wq_pre = [0]
    xpre = [0]
    htm_cons = [("sTPh", 0)] * 4
    xt_loads = {0: 0, 1: 0}
    xt_cons = {0: 0, 1: 0}
    out_cons = {0: 0, 1: 0}
    mvp_sc = {0: 0, 1: 0, 2: 0}
    w1_cons = [0] * 4
    w2_cons = [0] * 4

    P.oi("sync", "dma_start", out=bqkv_c[:], in_=d_bqkv, _s="sMS", _n=16)
    P.oi("sync", "dma_start", out=b1_c[:], in_=d_b1, _s="sMS", _n=16)
    P.oi("sync", "dma_start", out=gam_bc[:], in_=d_gam, _s="sMS", _n=16)
    P.oi("sync", "dma_start", out=bet_bc[:], in_=d_bet, _s="sMS", _n=16)
    P.oi("sync", "dma_start", out=pe2[:], in_=d_pe2, _s="sMS", _n=16)
    P.oi("sync", "dma_start", out=brow_f[0:1, :], in_=d_bout, _s="sMS", _n=16)
    P.oi("sync", "dma_start", out=brow_f[32:33, :], in_=d_b2, _s="sMS", _n=16)
    P.oi("sync", "dma_start", out=brow_f[64:65, :], in_=d_bv, _s="sMS", _n=16)

    for kc in range(8):
        P.oi("gpsimd", "dma_start",
             out=wqh[:, kc * 1536:(kc + 1) * 1536],
             in_=d_wqkv[kc * 128:(kc + 1) * 128, 0:1536], _s="sWQ", _n=16)
    P.oi("gpsimd", "dma_start", out=d_xbfT[:, 0:S], in_=d_xT[:, 0:S], _s="sXC0", _n=16)
    P.oi("gpsimd", "dma_start",
         out=wout_bf[:].rearrange("p (c d) -> p c d", c=8),
         in_=d_wout.rearrange("(c p) d -> p c d", p=128), _s="sWO", _n=16)
    P.oi("gpsimd", "dma_start", out=eye_sb[:], in_=d_eye, _s="sWO", _n=16)
    P.oi("gpsimd", "dma_start", out=d_wqbf, in_=d_wqkv[:, 1536:3072],
         _s="sWQ2", _n=16)

    P.o("vector", "memset", ones_bf[:], 1.0)
    P.o("vector", "memset", ones_f[:], 1.0)
    P.o("vector", "memset",
        v_tm[:].rearrange("p (t c) -> p t c", c=80)[:, :, 64:65], 1.0)
    P.w("vector", "sMS", 16 * 8)
    P.o("vector", "memset", brow_o[:], 0.0)
    P.o("vector", "memset", brow_2[:], 0.0)
    P.o("vector", "memset", brow_v[:], 0.0)
    P.d("vector")
    P.o("vector", "tensor_copy", brow_o[0:1, :], brow_f[0:1, :])
    P.d("vector")
    P.o("vector", "tensor_tensor", brow_o[64:65, :], brow_f[0:1, :],
        brow_o[0:1, :], op=ALU.subtract)
    P.o("vector", "tensor_scalar", brow_2[0:1, :], brow_f[32:33, :],
        0.5, None, op0=ALU.mult)
    P.o("vector", "tensor_scalar", x_tm2[0][0:1, :], brow_f[32:33, :],
        0.5, None, op0=ALU.mult)
    P.d("vector")
    P.o("vector", "tensor_tensor", brow_2[64:65, :], x_tm2[0][0:1, :],
        brow_2[0:1, :], op=ALU.subtract)
    P.o("vector", "tensor_copy", brow_v[0:1, :], brow_f[64:65, :])
    P.o("vector", "tensor_copy", x_tm2[1][0:1, :], brow_f[64:65, :])
    P.d("vector")
    vset = P.oi("vector", "tensor_tensor", brow_v[64:65, :], x_tm2[1][0:1, :],
                brow_v[0:1, :], op=ALU.subtract, _s="sV")

    dq_jobs = []
    for g in range(8):
        for cs in range(8):
            dq_jobs.append((d_q1[g * 128:(g + 1) * 128, cs * 512:(cs + 1) * 512],
                            d_s1[g * 128:(g + 1) * 128, cs * 512:(cs + 1) * 512],
                            d_w1s[g * 128:(g + 1) * 128, cs * 512:(cs + 1) * 512],
                            1.0, -128.0))
    for g in range(32):
        for cs in range(2):
            dq_jobs.append((d_q2[g * 128:(g + 1) * 128, cs * 512:(cs + 1) * 512],
                            d_s2[g * 128:(g + 1) * 128, cs * 512:(cs + 1) * 512],
                            d_w2s[g * 128:(g + 1) * 128, cs * 512:(cs + 1) * 512],
                            0.5, -64.0))
    DQ_TOTAL = len(dq_jobs)
    dq_i = [0]
    sc_slot = {0: 0, 1: 0}
    dq_ld = {0: 0, 1: 0}
    dq_wait = {0: 0, 1: 0}

    def dq_issue_loads(i):
        qa, sa, _, _, _ = dq_jobs[i]
        b = i % 2
        if i >= 2:
            P.w("gpsimd", f"sSC{b}", sc_slot[b])
        P.oi("gpsimd", "dma_start", out=dqq[b][:], in_=qa, _s=f"sDQ{b}", _n=16)
        dq_ld[b] = P.oi("gpsimd", "dma_start", out=dqs[b][:], in_=sa,
                        _s=f"sDQ{b}", _n=16)

    def pump_dequant(n):
        for _ in range(n):
            i = dq_i[0]
            if i >= DQ_TOTAL:
                return
            dq_i[0] += 1
            b = i % 2
            if i == 0:
                dq_issue_loads(0)
            wait_val = dq_ld[b]
            if i + 1 < DQ_TOTAL:
                dq_issue_loads(i + 1)
            _, _, wa, mul, off = dq_jobs[i]
            P.w("vector", f"sDQ{b}", wait_val)
            if mul == 1.0:
                gv = P.oi("vector", "scalar_tensor_tensor", dqw[b][:], dqq[b][:],
                          off, dqs[b][:], op0=ALU.add, op1=ALU.mult, _s="sG")
            else:
                P.o("vector", "tensor_scalar", dqt[b][:], dqq[b][:], mul, off,
                    op0=ALU.mult, op1=ALU.add)
                P.d("vector")
                gv = P.oi("vector", "tensor_mul", dqw[b][:], dqt[b][:], dqs[b][:],
                          _s="sG")
            P.w("gpsimd", "sG", gv)
            sc_slot[b] = P.oi("gpsimd", "dma_start", out=wa, in_=dqw[b][:],
                              _s=f"sSC{b}", _n=16)

    wq_pre[0] = P.val("sWQ")
    for e in ENGINES:
        P.w(e, "sMS", 16 * 8)
        P.w(e, "sWO", P.val("sWO"))
        P.w(e, "sV", vset)

    for b in range(NB):
        x_base = b * S
        vone = P.oi("vector", "memset",
                    v_tm[:].rearrange("p (t c) -> p t c", c=80)[:, :, 64:65], 1.0,
                    _s="sV")
        P.w("sync", "sV", vone)
        vcount = 0
        for half in range(2):
            if half == 0:
                wq_ready = wq_pre[0]
            else:
                P.w("gpsimd", "sT", P.val("sT"))
                P.w("gpsimd", "sWQ2", 16)
                for kc in range(8):
                    P.oi("gpsimd", "dma_start",
                         out=wqh[:, kc * 1536:(kc + 1) * 1536],
                         in_=d_wqbf[kc * 128:(kc + 1) * 128, :], _s="sWQ", _n=16)
                wq_ready = P.val("sWQ")
            if b == 0 and half == 0:
                P.oi("gpsimd", "dma_start", out=d_xbfT[:, S:2 * S],
                     in_=d_xT[:, S:2 * S], _s="sXC1", _n=16)
            def emit_x_transposes(t):
                xslot = t % 2
                sx = f"sTPx{xslot}"
                P.w("sync", "sT", xfm_cons[xslot])
                if half == 0 and t == 0:
                    P.w("sync", f"sXC{b}", 16)
                for c in range(8):
                    P.oi("sync", "dma_start",
                         out=x_fm[xslot][:, c * 512:(c + 1) * 512],
                         in_=d_xbfT[c * 128:(c + 1) * 128,
                                    x_base + t * 512: x_base + t * 512 + 512],
                         _s=sx, _n=16)
                xfm_ready[xslot] = P.val(sx)

            for t in range(NG):
                xslot = t % 2
                xf = x_fm[xslot]
                sx = f"sTPx{xslot}"
                if t == 0 and not (b == 1 and half == 0 and xpre[0]):
                    emit_x_transposes(0)
                if b == 1 and half == 0 and t == 0:
                    xpre[0] = 0
                if t + 1 < NG:
                    emit_x_transposes(t + 1)
                pump_dequant(3)
                P.w("tensor", sx, xfm_ready[xslot])
                if t == 0:
                    P.w("tensor", "sWQ", wq_ready)
                for mi in range(12 if half == 0 else 4):
                    m = half * 12 + mi
                    slot = (half * 48 + t * 12 + mi) % 4
                    ps = PS_B[:, slot * 512:(slot + 1) * 512]
                    sem, val = psBq_free[slot]
                    P.w("tensor", sem, val)
                    for kc in range(8):
                        rec = P.o("tensor", "matmul", ps,
                                  wqh[:, kc * 1536 + mi * 128: kc * 1536 + mi * 128 + 128],
                                  xf[:, kc * 512:(kc + 1) * 512],
                                  start=(kc == 0), stop=(kc == 7))
                    mmv = P.inc(rec, "sT")
                    if mi == 11 or (half == 1 and mi == 3):
                        pass
                    P.w("vector", "sT", mmv)
                    bias = bqkv_c[:, m:m + 1]
                    tcol = t * 512
                    if m < 8:
                        ev = P.oi("vector", "tensor_scalar",
                                  q_fm[:, m * S + tcol: m * S + tcol + 512], ps,
                                  bias, None, op0=ALU.add, _s="sV")
                    else:
                        ev = P.oi("vector", "scalar_tensor_tensor",
                                  k_fm[:, (m - 8) * S + tcol: (m - 8) * S + tcol + 512],
                                  ps, bias, pe2[:, tcol:tcol + 512],
                                  op0=ALU.add, op1=ALU.add, _s="sV")
                    psBq_free[slot] = ("sV", ev)
                if half == 1:
                    for sub in range(4):
                        ktile = t * 4 + sub
                        for vh in range(2):
                            vslot = (t * 8 + sub * 2 + vh) % 4
                            psv = PS_A[:, vslot * 512:(vslot + 1) * 512]
                            sem, val = psA_vfree[vslot]
                            P.w("tensor", sem, val)
                            for kc in range(8):
                                P.o("tensor", "matmul", psv,
                                    xf[:, kc * 512 + sub * 128: kc * 512 + sub * 128 + 128],
                                    wqh[:, kc * 1536 + 512 + vh * 512:
                                         kc * 1536 + 512 + vh * 512 + 512],
                                    start=(kc == 0), stop=False)
                            rec = P.o("tensor", "matmul", psv,
                                      ones_bf[0:65, 0:128],
                                      brow_v[0:65, vh * 512: vh * 512 + 512],
                                      start=False, stop=True)
                            mmv = P.inc(rec, "sT")
                            P.w("vector", "sT", mmv)
                            dst = v_tm[:, ktile * 1280 + vh * 8 * 80:
                                       ktile * 1280 + vh * 8 * 80 + 640]
                            ev = P.oi("vector", "tensor_copy",
                                      dst.rearrange("p (h d) -> p h d", d=80)[:, :, 0:64],
                                      psv.rearrange("p (h d) -> p h d", d=64),
                                      _s="sV")
                            psA_vfree[vslot] = ("sV", ev)
                if half == 1 and t == NG - 1:
                    xfm_cons[xslot] = P.val("sT")
                else:
                    xfm_cons[xslot] = P.val("sT")
        def norm_chunk(buf, c, ncol):
            sqb = sq_sb[ncol % 2]
            P.w("scalar", "sTPx0", sq_guard[0])
            P.w("scalar", "sTPx1", sq_guard[1])
            P.w("scalar", "sT", norm_mm_done[ncol % 2])
            sv = P.oi("scalar", "activation", sqb[:, :], buf[:, c * S:(c + 1) * S],
                      AF.Square, bias=f32(0.0), scale=f32(1.0), _s="sS")
            P.w("tensor", "sS", sv)
            for sl in range(4):
                ps = PS_A[0:1, sl * 512: sl * 512 + 512]
                sem, val = psA_norm_free[sl]
                P.w("tensor", sem, val)
                if sl == 3:
                    mmv = P.oi("tensor", "matmul", ps, ones_bf[:, 0:1],
                               sqb[:, sl * 512:(sl + 1) * 512],
                               start=True, stop=True, _s="sT")
                else:
                    rec = P.o("tensor", "matmul", ps, ones_bf[:, 0:1],
                              sqb[:, sl * 512:(sl + 1) * 512],
                              start=True, stop=True)
                    mmv = P.inc(rec, "sT")
                P.w("vector", "sT", mmv)
                rv = P.oi("vector", "reduce_max", nmax[0:1, 4 * ncol + sl: 4 * ncol + sl + 1],
                          ps, axis=AX.X, _s="sV")
                psA_norm_free[sl] = ("sV", rv)
            norm_mm_done[ncol % 2] = P.val("sT")

        sq_guard = {0: 0, 1: 0}
        norm_mm_done = {0: 0, 1: 0}
        psA_norm_free = list(psA_vfree)
        P.w("scalar", "sV", P.val("sV"))
        ncol = 0
        for buf in (q_fm, k_fm):
            for c in range(8):
                norm_chunk(buf, c, ncol)
                ncol += 1
        psA_free[0] = ("sV", P.val("sV"))
        psA_free[1] = ("sV", P.val("sV"))
        P.d("vector")
        P.o("vector", "reduce_max", nmax[0:1, 64:65], nmax[0:1, 0:64], axis=AX.X)
        P.d("vector")
        cv = P.oi("vector", "tensor_scalar", nmaxbf[0:1, 0:1], nmax[0:1, 64:65],
                  -0.125, None, op0=ALU.mult, _s="sV")
        P.w("tensor", "sV", cv)
        sem, val = psBq_free[0]
        P.w("tensor", sem, val)
        mmv = P.oi("tensor", "matmul", PS_B[:, 0:1], ones_bf[0:1, 0:128],
                   nmaxbf[0:1, 0:1], start=True, stop=True, _s="sT")
        P.w("vector", "sT", mmv)
        cbv = P.oi("vector", "tensor_copy", cneg[:], PS_B[:, 0:1], _s="sV")
        psBq_free[0] = ("sV", cbv)
        pump_dequant(8)
        pfx = "dbg_" if b == 0 else "dbg2_"
        dbg_dump([(pfx + "q", q_fm[:]), (pfx + "k", k_fm[:]), (pfx + "v", v_tm[:]),
                  (pfx + "cneg", cneg[:])])

        P.barrier()

        prev = [None]

        def emit_onorm_tensor():
            if prev[0] is None:
                return
            o_half_p, _, _, rv_p, bmv_hold = prev[0]
            P.w("tensor", "sV", rv_p)
            P.o("tensor", "matmul", o_half_p[64:128, 0:512], ones_f[64:65, 0:64],
                recb[64:65, 0:512], start=True, stop=True, tile_position=(64, 64))
            bmv_hold[0] = P.oi("tensor", "matmul", o_half_p[64:128, 512:1024],
                               ones_f[64:65, 0:64], recb[64:65, 512:1024],
                               start=True, stop=True, tile_position=(64, 64), _s="sT")

        def emit_onorm_vector():
            if prev[0] is None:
                return
            o_half_p, ocol_p, par_p, _, bmv_hold = prev[0]
            prev[0] = None
            P.w("vector", "sT", bmv_hold[0])
            P.o("vector", "tensor_copy", bc_sb[0:64, 0:512], o_half_p[64:128, 0:512])
            P.o("vector", "tensor_copy", bc_sb[64:128, 0:512],
                o_half_p[64:128, 512:1024])
            P.d("vector")
            P.o("vector", "tensor_tensor", O_all[0:64, ocol_p:ocol_p + 512],
                o_half_p[0:64, 0:512], bc_sb[0:64, 0:512], op=ALU.mult)
            ov = P.oi("vector", "tensor_tensor", O_all[64:128, ocol_p:ocol_p + 512],
                      o_half_p[0:64, 512:1024], bc_sb[64:128, 0:512],
                      op=ALU.mult, _s="sV")
            psB_half[par_p] = ("sV", ov)

        for g in range(NG):
            qcol = g * 512
            for p in range(PAIRS):
                par = p % 2
                o_half = PS_B[:, 0:1024] if par == 0 else PS_B[:, 1024:2048]
                exp_vals = []
                pv_done = 0

                def s_pair(kt):
                    s_ps = PS_A[:, 0:1024] if kt % 2 == 0 else PS_A[:, 1024:2048]
                    if kt >= 2:
                        P.w("tensor", "sS", exp_vals[kt - 2][1])
                    elif kt == 0:
                        P.w("tensor", psA_free[0][0], psA_free[0][1])
                        P.w("tensor", psA_free[1][0], psA_free[1][1])
                    rec = P.o("tensor", "matmul", s_ps[:, 0:512],
                              k_fm[0:64, p * S + kt * 128: p * S + kt * 128 + 128],
                              q_fm[0:64, p * S + qcol: p * S + qcol + 512],
                              start=True, stop=True, tile_position=(0, 0))
                    sv1 = P.inc(rec, "sT")
                    rec = P.o("tensor", "matmul", s_ps[:, 512:1024],
                              k_fm[64:128, p * S + kt * 128: p * S + kt * 128 + 128],
                              q_fm[64:128, p * S + qcol: p * S + qcol + 512],
                              start=True, stop=True, tile_position=(64, 0))
                    sv2 = P.inc(rec, "sT")
                    P.w("scalar", "sT", sv1)
                    ea = P.oi("scalar", "activation", pT[kt % 2][:, 0:512],
                              s_ps[:, 0:512], AF.Exp, bias=cneg[:, :],
                              scale=f32(0.125), _s="sS")
                    P.w("scalar", "sT", sv2)
                    eb = P.oi("scalar", "activation", pT[kt % 2][:, 512:1024],
                              s_ps[:, 512:1024], AF.Exp, bias=cneg[:, :],
                              scale=f32(0.125), _s="sS")
                    exp_vals.append((ea, eb))

                s_pair(0)
                for kt in range(KT):
                    if kt + 1 < KT:
                        s_pair(kt + 1)
                    if kt == 6:
                        emit_onorm_tensor()
                    P.w("tensor", "sS", exp_vals[kt][0])
                    if kt == 0:
                        P.w("tensor", psB_half[par][0], psB_half[par][1])
                    P.o("tensor", "matmul", o_half[0:65, 0:512],
                        v_tm[:, kt * 1280 + 2 * p * 80: kt * 1280 + 2 * p * 80 + 65],
                        pT[kt % 2][:, 0:512], start=(kt == 0), stop=(kt == KT - 1))
                    P.w("tensor", "sS", exp_vals[kt][1])
                    rec = P.o("tensor", "matmul", o_half[0:65, 512:1024],
                              v_tm[:, kt * 1280 + (2 * p + 1) * 80:
                                   kt * 1280 + (2 * p + 1) * 80 + 65],
                              pT[kt % 2][:, 512:1024],
                              start=(kt == 0), stop=(kt == KT - 1))
                    if kt == KT - 1:
                        pv_done = P.inc(rec, "sT")
                psA_free[0] = ("sS", exp_vals[KT - 2][1])
                psA_free[1] = ("sS", exp_vals[KT - 1][1])
                emit_onorm_vector()
                P.w("vector", "sT", pv_done)
                P.o("vector", "reciprocal", recb[64:65, 0:512], o_half[64:65, 0:512])
                rv = P.oi("vector", "reciprocal", recb[64:65, 512:1024],
                          o_half[64:65, 512:1024], _s="sV")
                prev[0] = (o_half, p * S + qcol, par, rv, [0])
                pump_dequant(3)
        emit_onorm_tensor()
        emit_onorm_vector()
        dbg_dump([(("dbg_" if b == 0 else "dbg2_") + "O", O_all[:])])

        P.barrier()

        fc2_banks = [PS_A[:, 0:1024], PS_A[:, 1024:2048],
                     PS_B[:, 0:1024], PS_B[:, 1024:2048]]
        fc2_free = [("sV", 0)] * 4
        psBq_free = [psB_half[0], psB_half[0], psB_half[1], psB_half[1]]
        fc1_mm = {}
        uv_j = {}
        hv_j = {}
        chain_agv = {}
        chain_ag2 = {}
        oall_done = [0]
        hpe_last = [0]

        w1ring = w1buf + [w1buf_x]
        w2ring = w2buf + [w2buf_x]

        def xt_prefetch(g2, j):
            row = x_base + g2 * 512 + j * 128
            P.w("sync", "sV", xt_cons[j % 2])
            xt_loads[j % 2] = P.oi("sync", "dma_start", out=x_tm2[j % 2][:],
                                   in_=x_rows[row:row + 128, :],
                                   _s=f"sXT{j % 2}", _n=16)

        def partA_tv(g2, j):
            ps_u = PS_A[:, (j % 2) * 1024:(j % 2) * 1024 + 1024]
            x1j = x1buf[g2 % 2][:, j * DM:(j + 1) * DM]
            P.w("tensor", psA_free[j % 2][0], psA_free[j % 2][1])
            for hf in range(2):
                for p in range(8):
                    P.o("tensor", "matmul", ps_u[:, hf * 512:(hf + 1) * 512],
                        O_all[:, p * S + g2 * 512 + j * 128:
                              p * S + g2 * 512 + j * 128 + 128],
                        wout_bf[:, p * 1024 + hf * 512: p * 1024 + (hf + 1) * 512],
                        start=(p == 0), stop=False)
                rec = P.o("tensor", "matmul", ps_u[:, hf * 512:(hf + 1) * 512],
                          ones_bf[0:65, 0:128],
                          brow_o[0:65, hf * 512: hf * 512 + 512],
                          start=False, stop=True)
            opv = P.inc(rec, "sT")
            if g2 == 3 and j == 3:
                oall_done[0] = opv
            P.w("vector", "sT", opv)
            P.w("vector", f"sXT{j % 2}", xt_loads[j % 2])
            uv = P.oi("vector", "tensor_tensor", x1j, ps_u, x_tm2[j % 2][:],
                      op=ALU.add, _s="sV")
            uv_j[j] = uv
            psA_free[j % 2] = ("sV", uv)
            xt_cons[j % 2] = uv
            if j + 2 < 4:
                xt_prefetch(g2, j + 2)
            P.d("vector")
            P.o("vector", "bn_stats", stats[:, 0:6], x1j[:, 0:512])
            P.o("vector", "bn_stats", stats[:, 6:12], x1j[:, 512:1024])
            P.d("vector")
            chain_agv[j] = P.oi("vector", "bn_aggr", mvp[:, 3 * j:3 * j + 2],
                                stats[:, 0:12], _s="sV")

        def batch_scalarA():
            P.w("scalar", "sV", chain_agv[3])
            for j in range(4):
                P.o("scalar", "activation", mvp[:, 3 * j + 2:3 * j + 3],
                    mvp[:, 3 * j + 1:3 * j + 2], AF.Ln, bias=f32(EPS), scale=f32(1.0))
                P.o("scalar", "activation", mvp[:, 12 + 3 * j + 2:12 + 3 * j + 3],
                    mvp[:, 3 * j + 1:3 * j + 2], AF.Ln, bias=f32(EPS * EPS),
                    scale=f32(1.0 + EPS))
            P.d("scalar")
            for j in range(4):
                P.o("scalar", "activation", mvp[:, 3 * j + 2:3 * j + 3],
                    mvp[:, 3 * j + 2:3 * j + 3], AF.Exp, bias=f32(0.0),
                    scale=f32(-0.5))
                rec = P.o("scalar", "activation", mvp[:, 12 + 3 * j + 2:12 + 3 * j + 3],
                          mvp[:, 12 + 3 * j + 2:12 + 3 * j + 3], AF.Exp, bias=f32(0.0),
                          scale=f32(-0.5))
            mvp_sc[0] = P.inc(rec, "sS")

        def applyAB(g2, j):
            x1j = x1buf[g2 % 2][:, j * DM:(j + 1) * DM]
            c0 = 3 * j
            c1 = 12 + 3 * j
            P.w("vector", "sS", mvp_sc[0])
            P.o("vector", "tensor_scalar", t2_sb[:], x1j, mvp[:, c0:c0 + 1],
                mvp[:, c1 + 2:c1 + 3], op0=ALU.subtract, op1=ALU.mult)
            P.d("vector")
            P.o("vector", "tensor_scalar", x1j, x1j, mvp[:, c0:c0 + 1],
                mvp[:, c0 + 2:c0 + 3], op0=ALU.subtract, op1=ALU.mult)
            P.o("vector", "tensor_tensor", u_sb[:], t2_sb[:], gam_bc[:], op=ALU.mult)
            P.d("vector")
            P.w("vector", htm_cons[j][0], htm_cons[j][1])
            hv_j[j] = P.oi("vector", "tensor_tensor", h_tm[j][:], u_sb[:], bet_bc[:],
                           op=ALU.add, _s="sV")

        def h_transposes_pe(j):
            half = PS_A[:, (j % 2) * 1024:(j % 2) * 1024 + 1024]
            P.w("tensor", "sV", hv_j[j])
            P.w("tensor", psA_free[j % 2][0], psA_free[j % 2][1])
            for c in range(8):
                rec = P.o("tensor", "matmul", half[:, c * 128:(c + 1) * 128],
                          h_tm[j][:, c * 128:(c + 1) * 128], eye_sb[:],
                          start=True, stop=True)
            mmv = P.inc(rec, "sT")
            htm_cons[j] = ("sT", mmv)
            P.w("scalar", "sT", mmv)
            for c in range(8):
                rec2 = P.o("scalar", "activation",
                           h_fm[:, c * 512 + j * 128: c * 512 + j * 128 + 128],
                           half[:, c * 128:(c + 1) * 128], AF.Copy,
                           bias=0.0, scale=1.0)
            ev = P.inc(rec2, "sS")
            psA_free[j % 2] = ("sS", ev)
            hpe_last[0] = ev

        def h_transposes(g2, j, also_wait_fc1=0):
            P.w("sync", "sV", hv_j[j])
            if also_wait_fc1:
                P.w("sync", "sT", also_wait_fc1)
            for c in range(8):
                P.oi("sync", "dma_start",
                     out=h_fm[:, c * 512 + j * 128: c * 512 + j * 128 + 128],
                     in_=h_tm[j][:, c * 128:(c + 1) * 128], transpose=True,
                     _s="sTPh", _n=16)
            htm_cons[j] = ("sTPh", P.val("sTPh"))

        partA_tv_at = {3: 0, 7: 1, 11: 2, 15: 3}
        for g in range(NG):
            if g == 0:
                xt_prefetch(0, 0)
                xt_prefetch(0, 1)
                for j in range(4):
                    partA_tv(0, j)
                batch_scalarA()
                for j in range(4):
                    applyAB(0, j)
                    h_transposes_pe(j)
            if g == 0:
                P.w("tensor", "sS", hpe_last[0])
            else:
                P.w("tensor", "sTPh", P.val("sTPh"))
            if b == 0 and g == 0:
                pump_dequant(DQ_TOTAL)
                assert dq_i[0] == DQ_TOTAL
                P.w("gpsimd", "sSC0", sc_slot[0])
                P.w("gpsimd", "sSC1", sc_slot[1])
            silu_vals = {}
            for f in range(32):
                wb3 = f % 3
                sw = "sW1" + "abc"[wb3]
                P.w("gpsimd", "sT", w1_cons[wb3])
                wv1 = P.oi("gpsimd", "dma_start",
                           out=w1ring[wb3][:].rearrange("p (k m) -> p k m", k=8),
                           in_=d_w1s[:, f * 128:(f + 1) * 128].rearrange(
                               "(k p) m -> p k m", p=128), _s=sw, _n=16)
                P.w("tensor", sw, wv1)
                ps = PS_B[:, (f % 4) * 512:(f % 4) * 512 + 512]
                if f >= 4:
                    P.w("tensor", "sS", silu_vals[f - 4])
                else:
                    P.w("tensor", psBq_free[f % 4][0], psBq_free[f % 4][1])
                for kc in range(8):
                    rec = P.o("tensor", "matmul", ps,
                              w1ring[wb3][:, kc * 128:(kc + 1) * 128],
                              h_fm[:, kc * 512:(kc + 1) * 512],
                              start=(kc == 0), stop=(kc == 7))
                fc1_mm[f] = w1_cons[wb3] = P.inc(rec, "sT")
                P.w("scalar", "sT", fc1_mm[f])
                silu_vals[f] = P.oi("scalar", "activation",
                                    y_g[:, f * 512:(f + 1) * 512], ps, AF.Silu,
                                    bias=b1_c[:, f:f + 1], scale=f32(1.0), _s="sS")
                if g + 1 < NG:
                    if f == 1:
                        xt_prefetch(g + 1, 0)
                        xt_prefetch(g + 1, 1)
                    if f in partA_tv_at:
                        partA_tv(g + 1, partA_tv_at[f])
                    if f == 17:
                        batch_scalarA()
                    if f in (19, 22, 25, 28):
                        applyAB(g + 1, (f - 19) // 3)
            for sl in range(4):
                psBq_free[sl] = ("sS", silu_vals[28 + sl])
            for f in range(32):
                wb3 = f % 3
                sw = "sW2" + "abc"[wb3]
                P.w("gpsimd", "sT", w2_cons[wb3])
                wv2 = P.oi("gpsimd", "dma_start", out=w2ring[wb3][:],
                           in_=d_w2s[f * 128:(f + 1) * 128, :], _s=sw, _n=16)
                P.w("tensor", sw, wv2)
                if f == 0:
                    P.w("tensor", "sS", silu_vals[31])
                    if g + 1 < NG:
                        P.w("tensor", "sV", uv_j[2])
                        P.w("tensor", "sV", uv_j[3])
                    for jj in range(4):
                        P.w("tensor", fc2_free[jj][0], fc2_free[jj][1])
                for j in range(4):
                    for hf in range(2):
                        rec = P.o("tensor", "matmul",
                                  fc2_banks[j][:, hf * 512:(hf + 1) * 512],
                                  y_g[:, f * 512 + j * 128: f * 512 + j * 128 + 128],
                                  w2ring[wb3][:, hf * 512:(hf + 1) * 512],
                                  start=(f == 0), stop=False)
                w2_cons[wb3] = P.inc(rec, "sT")
                if g + 1 < NG and f in (1, 4, 7, 10):
                    h_transposes(g + 1, (f - 1) // 3,
                                 also_wait_fc1=fc1_mm[31] if f == 1 else 0)
            for j in range(4):
                for hf in range(2):
                    rec = P.o("tensor", "matmul",
                              fc2_banks[j][:, hf * 512:(hf + 1) * 512],
                              ones_bf[0:65, 0:128],
                              brow_2[0:65, hf * 512: hf * 512 + 512],
                              start=False, stop=True)
            z_done = P.inc(rec, "sT")
            for j in (2, 3, 0, 1):
                ob = j % 2
                P.w("vector", "sT", z_done)
                P.w("vector", f"sOUT{ob}", out_cons[ob])
                x1j = x1buf[g % 2][:, j * DM:(j + 1) * DM]
                tv = P.oi("vector", "tensor_tensor", t2_sb[:], fc2_banks[j], x1j,
                          op=ALU.add, _s="sV")
                fc2_free[j] = ("sV", tv)
                P.d("vector")
                P.o("vector", "bn_stats", stats3[:, 0:6], t2_sb[:, 0:512])
                P.o("vector", "bn_stats", stats3[:, 6:12], t2_sb[:, 512:1024])
                P.d("vector")
                P.w("vector", "sS", mvp_sc[2])
                agv = P.oi("vector", "bn_aggr", mvp2[:, 0:2], stats3[:, 0:12], _s="sV")
                P.w("scalar", "sV", agv)
                P.o("scalar", "activation", mvp2[:, 2:3], mvp2[:, 1:2], AF.Ln,
                    bias=f32(EPS), scale=f32(1.0))
                P.d("scalar")
                mvp_sc[2] = P.oi("scalar", "activation", mvp2[:, 2:3], mvp2[:, 2:3],
                                 AF.Exp, bias=f32(0.0), scale=f32(-0.5), _s="sS")
                P.w("vector", "sS", mvp_sc[2])
                wv = P.oi("vector", "tensor_scalar", out_sb[ob][:], t2_sb[:],
                          mvp2[:, 0:1], mvp2[:, 2:3], op0=ALU.subtract, op1=ALU.mult,
                          _s="sV")
                P.w("sync", "sV", wv)
                row = x_base + g * 512 + j * 128
                out_cons[ob] = P.oi("sync", "dma_start", out=y_rows[row:row + 128, :],
                                    in_=out_sb[ob][:], _s=f"sOUT{ob}", _n=16)
            psA_free[0] = fc2_free[0]
            psA_free[1] = fc2_free[1]
            psBq_free[0] = psBq_free[1] = fc2_free[2]
            psBq_free[2] = psBq_free[3] = fc2_free[3]
            if g == 0:
                pfx2 = "dbg_" if b == 0 else "dbg2_"
                dbg_dump([(pfx2 + "x1", x1buf[0][:]), (pfx2 + "h", h_fm[:]),
                          (pfx2 + "y", y_g[:])])
        psB_half = {0: fc2_free[2], 1: fc2_free[3]}
        if b == 0:
            P.w("gpsimd", "sT", oall_done[0])
            for kc in range(8):
                P.oi("gpsimd", "dma_start",
                     out=wqh[:, kc * 1536:(kc + 1) * 1536],
                     in_=d_wqkv[kc * 128:(kc + 1) * 128, 0:1536], _s="sWQ", _n=16)
            wq_pre[0] = P.val("sWQ")
            P.w("sync", "sT", max(oall_done[0], xfm_cons[0]))
            P.w("sync", "sXC1", 16)
            for c in range(8):
                P.oi("sync", "dma_start",
                     out=x_fm[0][:, c * 512:(c + 1) * 512],
                     in_=d_xbfT[c * 128:(c + 1) * 128, S: S + 512],
                     _s="sTPx0", _n=16)
            xfm_ready[0] = P.val("sTPx0")
            xpre[0] = 1
        P.barrier()

    assert dq_i[0] == DQ_TOTAL
    assert debug or P.val("sOUT0") + P.val("sOUT1") == 16 * NB * 16
    P.w("sync", "sOUT0", P.val("sOUT0"))
    P.w("sync", "sOUT1", P.val("sOUT1"))

    def run_ops(e, eng_name):
        for rec in P.ops[eng_name]:
            if rec[0] == "w":
                e.wait_ge(sems[rec[1]], rec[2])
            elif rec[0] == "d":
                e.drain()
            else:
                ins = getattr(e, rec[1])(*rec[2], **rec[3])
                if rec[4] is not None:
                    ins.then_inc(sems[rec[4][0]], rec[4][1])

    with es, nc.Block() as block:
        @block.sync
        def _(e):
            run_ops(e, "sync")

        @block.gpsimd
        def _(e):
            run_ops(e, "gpsimd")

        @block.tensor
        def _(e):
            run_ops(e, "tensor")

        @block.vector
        def _(e):
            run_ops(e, "vector")

        @block.scalar
        def _(e):
            run_ops(e, "scalar")

    return nc


def _pos_enc():
    positions = np.arange(S, dtype=np.float32)[:, None]
    div_term = np.exp(np.arange(0, 64, 2, dtype=np.float32)
                      * f32(-(np.log(10000.0) / 64)))
    pe = np.zeros((S, 64), dtype=np.float32)
    pe[:, 0::2] = np.sin(positions * div_term)
    pe[:, 1::2] = np.cos(positions * div_term)
    return pe


_return_in_maps = [False]


def kernel(L0_attn_x, L0_attn_W_qkv, L0_attn_b_qkv, L0_attn_W_out, L0_attn_b_out,
           L0_ffn_ln_gamma, L0_ffn_ln_beta, L0_ffn_fc1_q, L0_ffn_fc1_scales,
           L0_ffn_fc1_bias, L0_ffn_fc2_q, L0_ffn_fc2_scales, L0_ffn_fc2_bias):
    x = np.ascontiguousarray(np.asarray(L0_attn_x, dtype=np.float32))
    wqkv = np.ascontiguousarray(np.asarray(L0_attn_W_qkv, dtype=np.float32))
    bqkv = np.asarray(L0_attn_b_qkv, dtype=np.float32)
    wout = np.ascontiguousarray(np.asarray(L0_attn_W_out, dtype=np.float32))
    bout = np.asarray(L0_attn_b_out, dtype=np.float32)
    gam = np.asarray(L0_ffn_ln_gamma, dtype=np.float32)
    bet = np.asarray(L0_ffn_ln_beta, dtype=np.float32)
    q1 = np.ascontiguousarray(np.asarray(L0_ffn_fc1_q, dtype=np.int32))
    s1 = np.asarray(L0_ffn_fc1_scales, dtype=np.float32)
    b1 = np.asarray(L0_ffn_fc1_bias, dtype=np.float32)
    q2 = np.ascontiguousarray(np.asarray(L0_ffn_fc2_q, dtype=np.int32))
    s2 = np.asarray(L0_ffn_fc2_scales, dtype=np.float32)
    b2 = np.asarray(L0_ffn_fc2_bias, dtype=np.float32)

    pe = _pos_enc()
    shared = dict(
        wqkv=wqkv,
        wout=wout,
        bqkv_c=np.ascontiguousarray(bqkv.reshape(24, 128).T),
        bout_r=np.ascontiguousarray(bout.reshape(1, DM)),
        bv_r=np.ascontiguousarray(bqkv[2048:3072].reshape(1, DM)),
        gam_bc=np.ascontiguousarray(np.broadcast_to(gam[None, :], (128, DM))),
        bet_bc=np.ascontiguousarray(np.broadcast_to(bet[None, :], (128, DM))),
        q1=q1,
        s1bc=np.ascontiguousarray(
            np.broadcast_to(s1[:, None, :], (8, 128, FF)).reshape(DM, FF)),
        b1_c=np.ascontiguousarray(b1.reshape(32, 128).T),
        q2=q2,
        s2bc=np.ascontiguousarray(
            np.broadcast_to(s2[:, None, :], (32, 128, DM)).reshape(FF, DM)),
        b2_r=np.ascontiguousarray(b2.reshape(1, DM)),
        pe2=np.ascontiguousarray(np.concatenate([pe.T, pe.T], axis=0)),
    )
    in_maps = []
    for c in range(8):
        m = dict(shared)
        m["x"] = np.ascontiguousarray(x[c * NB:(c + 1) * NB])
        m["xt"] = np.ascontiguousarray(
            x[c * NB:(c + 1) * NB].reshape(NB * 2048, 1024).T)
        m["eye"] = np.eye(128, dtype=np.float32).astype(ml_dtypes.bfloat16)
        in_maps.append(m)
    if _return_in_maps[0]:
        return in_maps

    nc = build_program()
    res = run_bass_kernel_spmd(nc, in_maps, core_ids=list(range(8)))
    out = np.concatenate([np.asarray(r["y"], dtype=np.float32) for r in res.results],
                         axis=0)
    return out


# version 54
# speedup vs baseline: 1.1465x; 1.1465x over previous
import numpy as np
import ml_dtypes
import contextlib
import concourse.bass as bass
import concourse.mybir as mybir
from concourse.bass_utils import run_bass_kernel_spmd

F32 = mybir.dt.float32
BF16 = mybir.dt.bfloat16
I32 = mybir.dt.int32
AF = mybir.ActivationFunctionType
ALU = mybir.AluOpType
AX = mybir.AxisListType
f32 = np.float32

NB = 2
S = 2048
DM = 1024
H = 16
FF = 4096
NG = 4
KT = 16
PAIRS = 8
EPS = 1e-5

ENGINES = ("sync", "gpsimd", "tensor", "vector", "scalar")


class Plan:

    def __init__(self):
        self.ops = {e: [] for e in ENGINES}
        self.sem = {}

    def val(self, s):
        return self.sem.get(s, 0)

    def w(self, eng, s, v):
        if v > 0:
            self.ops[eng].append(("w", s, v))

    def d(self, eng):
        self.ops[eng].append(("d",))

    def o(self, eng, method, *args, **kw):
        rec = ["o", method, args, kw, None]
        self.ops[eng].append(rec)
        return rec

    def inc(self, rec, s, n=1):
        assert rec[4] is None
        self.sem[s] = self.sem.get(s, 0) + n
        rec[4] = (s, n)
        return self.sem[s]

    def oi(self, eng, method, *args, _s=None, _n=1, **kw):
        rec = self.o(eng, method, *args, **kw)
        self.inc(rec, _s, _n)
        return self.sem[_s]

    def hinc(self, eng, s):
        self.d(eng)
        m = "engine_nop" if eng in ("vector", "gpsimd") else "nop"
        return self.oi(eng, m, _s=s)

    BARRIER_EXCLUDE = ("sSC0", "sSC1", "sDQ0", "sDQ1", "sG")

    def barrier(self):
        snap = dict(self.sem)
        for e in ENGINES:
            for s, v in snap.items():
                if v > 0 and s not in self.BARRIER_EXCLUDE:
                    self.ops[e].append(("w", s, v))


def build_program(debug=False):
    nc = bass.Bass("TRN2", target_bir_lowering=False)

    d_x = nc.dram_tensor("x", [NB, S, DM], F32, kind="ExternalInput").ap()
    d_xT = nc.dram_tensor("xt", [DM, NB * S], F32, kind="ExternalInput").ap()
    d_eye = nc.dram_tensor("eye", [128, 128], BF16, kind="ExternalInput").ap()
    d_wqbf = nc.dram_tensor("wqbf", [1024, 1536], BF16, kind="Internal").ap()
    d_wqkv = nc.dram_tensor("wqkv", [DM, 3 * DM], F32, kind="ExternalInput").ap()
    d_wout = nc.dram_tensor("wout", [DM, DM], F32, kind="ExternalInput").ap()
    d_bqkv = nc.dram_tensor("bqkv_c", [128, 24], F32, kind="ExternalInput").ap()
    d_bout = nc.dram_tensor("bout_r", [1, DM], F32, kind="ExternalInput").ap()
    d_bv = nc.dram_tensor("bv_r", [1, DM], F32, kind="ExternalInput").ap()
    d_gam = nc.dram_tensor("gam_bc", [128, DM], F32, kind="ExternalInput").ap()
    d_bet = nc.dram_tensor("bet_bc", [128, DM], F32, kind="ExternalInput").ap()
    d_q1 = nc.dram_tensor("q1", [DM, FF], I32, kind="ExternalInput").ap()
    d_s1 = nc.dram_tensor("s1bc", [DM, FF], F32, kind="ExternalInput").ap()
    d_b1 = nc.dram_tensor("b1_c", [128, 32], F32, kind="ExternalInput").ap()
    d_q2 = nc.dram_tensor("q2", [FF, DM], I32, kind="ExternalInput").ap()
    d_s2 = nc.dram_tensor("s2bc", [FF, DM], F32, kind="ExternalInput").ap()
    d_b2 = nc.dram_tensor("b2_r", [1, DM], F32, kind="ExternalInput").ap()
    d_pe2 = nc.dram_tensor("pe2", [128, S], F32, kind="ExternalInput").ap()
    d_y = nc.dram_tensor("y", [NB, S, DM], F32, kind="ExternalOutput").ap()
    d_w1s = nc.dram_tensor("w1s", [DM, FF], BF16, kind="Internal").ap()
    d_w2s = nc.dram_tensor("w2s", [FF, DM], BF16, kind="Internal").ap()
    d_xbfT = nc.dram_tensor("xbft", [DM, NB * S], BF16, kind="Internal").ap()

    x_rows = d_x.rearrange("b s d -> (b s) d")
    y_rows = d_y.rearrange("b s d -> (b s) d")

    dbg_t = {}
    if debug:
        for nm, shp, dt in (("dbg_q", [128, 8 * S], BF16), ("dbg_k", [128, 8 * S], BF16),
                            ("dbg_v", [128, KT * H * 80], BF16), ("dbg_cneg", [128, 1], F32),
                            ("dbg_O", [128, 8 * S], BF16), ("dbg_x1", [128, 4 * DM], F32),
                            ("dbg_h", [128, 8 * 512], BF16), ("dbg_y", [128, 32 * 512], BF16),
                            ("dbg2_q", [128, 8 * S], BF16), ("dbg2_k", [128, 8 * S], BF16),
                            ("dbg2_v", [128, KT * H * 80], BF16), ("dbg2_cneg", [128, 1], F32),
                            ("dbg2_O", [128, 8 * S], BF16), ("dbg2_x1", [128, 4 * DM], F32),
                            ("dbg2_h", [128, 8 * 512], BF16), ("dbg2_y", [128, 32 * 512], BF16)):
            dbg_t[nm] = nc.dram_tensor(nm, shp, dt, kind="ExternalOutput").ap()

    SB_BASE = 16512
    SBUF_LIMIT = 229344
    alloc = nc.alloc_sbuf_tensor_at
    cur = [SB_BASE]

    def place(name, shape, dt, at=None):
        if at is None:
            at = cur[0]
        nbytes = int(np.prod(shape[1:])) * mybir.dt.size(dt)
        t = alloc(name, shape, dt, offset=at)
        end = (at + nbytes + 31) // 32 * 32
        cur[0] = max(cur[0], end)
        assert cur[0] <= SBUF_LIMIT, (name, cur[0])
        return t

    ones_bf = place("ones_bf", [128, 128], BF16)
    ones_f = place("ones_f", [128, 64], F32)
    bqkv_c = place("bqkv_c", [128, 24], F32)
    b1_c = place("b1_c", [128, 32], F32)
    cneg = place("cneg", [128, 1], F32)
    nmax = place("nmax", [1, 72], F32)
    nmaxbf = place("nmaxbf", [1, 8], BF16)
    stats = place("stats", [128, 12], F32)
    stats2 = place("stats2", [128, 12], F32)
    stats3 = place("stats3", [128, 12], F32)
    mvp = place("mvp", [128, 24], F32)
    mvp2 = place("mvp2", [128, 4], F32)
    brow_o = place("brow_o", [128, 1024], BF16)
    brow_2 = place("brow_2", [128, 1024], BF16)
    brow_v = place("brow_v", [128, 1024], BF16)
    w1buf_x = place("w1buf2", [128, 1024], BF16)
    gam_bc = place("gam_bc", [128, DM], F32)
    bet_bc = place("bet_bc", [128, DM], F32)
    wout_bf = place("wout_bf", [128, 8 * DM], BF16)
    pe2 = place("pe2", [128, S], F32)
    assert cur[0] <= SB_BASE + 43008, cur[0]

    B0 = SB_BASE + 43008
    q_fm = place("q_fm", [128, 8 * S], BF16, at=B0)
    k_fm = place("k_fm", [128, 8 * S], BF16, at=B0 + 32768)
    v_tm = place("v_tm", [128, KT * H * 80], BF16, at=B0 + 65536)
    B_END = B0 + 65536 + 40960
    x_tm2 = [place(f"x_tm2_{i}", [128, DM], F32, at=B0 + i * 4096) for i in range(2)]
    u_sb = place("u_sb", [128, DM], F32, at=B0 + 8192)
    x1buf = [place(f"x1_{i}", [128, 4 * DM], F32, at=B0 + 12288 + i * 16384)
             for i in range(2)]
    t2_sb = place("t2_sb", [128, DM], F32, at=B0 + 45056)
    h_tm = [place(f"h_tm{i}", [128, DM], BF16, at=B0 + 49152 + i * 2048)
            for i in range(4)]
    h_fm = place("h_fm", [128, 8 * 512], BF16, at=B0 + 57344)
    y_g = place("y_g", [128, 32 * 512], BF16, at=B0 + 65536)
    w1buf = [place(f"w1buf{i}", [128, 1024], BF16, at=B0 + 98304 + i * 2048)
             for i in range(2)]
    w2buf = [place(f"w2buf{i}", [128, 1024], BF16, at=B0 + 102400 + i * 2048)
             for i in range(2)]
    assert B0 + 102400 + 4096 <= B_END

    C0 = B_END
    wqh = place("wqh", [128, 8 * 1536], BF16, at=C0)
    x_fm = [place(f"x_fm{i}", [128, 8 * 512], BF16, at=C0 + 24576 + i * 8192)
            for i in range(2)]
    sq_sb = [place(f"sq_sb{i}", [128, S], BF16, at=C0 + 24576 + i * 8192)
             for i in range(2)]
    O_all = place("O_all", [128, 8 * S], BF16, at=C0)
    pT = [place(f"pT{i}", [128, 1024], BF16, at=C0 + 32768 + i * 2048) for i in range(2)]
    recb = place("recb", [128, 1024], F32, at=C0 + 36864)
    bc_sb = place("bc_sb", [128, 1024], F32, at=C0 + 40960)
    C_END = C0 + 45056
    out_sb = [place(f"out_sb{i}", [128, DM], F32, at=C0 + 36864 + i * 4096)
              for i in range(2)]

    D0 = C_END
    dqq = [place(f"dqq{i}", [128, 512], I32, at=D0 + i * 2048) for i in range(2)]
    dqs = [place(f"dqs{i}", [128, 512], F32, at=D0 + 4096 + i * 2048) for i in range(2)]
    dqt = [place(f"dqt{i}", [128, 512], F32, at=D0 + 8192 + i * 2048) for i in range(2)]
    dqw = [place(f"dqw{i}", [128, 512], BF16, at=D0 + 12288 + i * 1024) for i in range(2)]
    brow_f = place("brow_f", [128, 1024], F32, at=C0 + 36864)
    w2buf_x = place("w2buf2", [128, 1024], BF16, at=D0 + 14336)
    eye_sb = place("eye_sb", [128, 128], BF16, at=D0 + 16384)
    assert D0 + 16640 <= SBUF_LIMIT, D0 + 16640

    PS_A = nc.alloc_psum_tensor("PS_A", [128, 2048], F32)
    PS_B = nc.alloc_psum_tensor("PS_B", [128, 2048], F32)

    es = contextlib.ExitStack()
    sems = {}
    for s in ("sXC0", "sXC1", "sWQ", "sWO", "sXT0", "sXT1",
              "sW1a", "sW1b", "sW1c", "sW2a", "sW2b", "sW2c",
              "sDQ0", "sDQ1", "sSC0", "sSC1", "sOUT0", "sOUT1", "sMS",
              "sTPx0", "sTPx1", "sTPv0", "sTPv1", "sTPh", "sT", "sV", "sS", "sG",
              "sWQ2"):
        sems[s] = es.enter_context(nc.semaphore(s))

    P = Plan()

    def dbg_dump(pairs):
        if not debug:
            return
        P.barrier()
        for nm, ap in pairs:
            P.oi("sync", "dma_start", out=dbg_t[nm], in_=ap, _s="sOUT0", _n=16)
        P.barrier()

    psA_free = {0: ("sV", 0), 1: ("sV", 0)}
    psB_half = {0: ("sV", 0), 1: ("sV", 0)}
    psBq_free = [("sV", 0)] * 4
    vtmp_cons = {0: 0, 1: 0}
    xfm_cons = {0: 0, 1: 0}
    xfm_ready = {0: 0, 1: 0}
    psA_vfree = [("sV", 0)] * 4
    wq_pre = [0]
    xpre = [0]
    htm_cons = [("sTPh", 0)] * 4
    xt_loads = {0: 0, 1: 0}
    xt_cons = {0: 0, 1: 0}
    out_cons = {0: 0, 1: 0}
    mvp_sc = {0: 0, 1: 0, 2: 0}
    w1_cons = [0] * 4
    w2_cons = [0] * 4

    P.oi("sync", "dma_start", out=bqkv_c[:], in_=d_bqkv, _s="sMS", _n=16)
    P.oi("sync", "dma_start", out=b1_c[:], in_=d_b1, _s="sMS", _n=16)
    P.oi("sync", "dma_start", out=gam_bc[:], in_=d_gam, _s="sMS", _n=16)
    P.oi("sync", "dma_start", out=bet_bc[:], in_=d_bet, _s="sMS", _n=16)
    P.oi("sync", "dma_start", out=pe2[:], in_=d_pe2, _s="sMS", _n=16)
    P.oi("sync", "dma_start", out=brow_f[0:1, :], in_=d_bout, _s="sMS", _n=16)
    P.oi("sync", "dma_start", out=brow_f[32:33, :], in_=d_b2, _s="sMS", _n=16)
    P.oi("sync", "dma_start", out=brow_f[64:65, :], in_=d_bv, _s="sMS", _n=16)

    for kc in range(8):
        P.oi("gpsimd", "dma_start",
             out=wqh[:, kc * 1536:(kc + 1) * 1536],
             in_=d_wqkv[kc * 128:(kc + 1) * 128, 0:1536], _s="sWQ", _n=16)
    P.oi("gpsimd", "dma_start", out=d_xbfT[:, 0:S], in_=d_xT[:, 0:S], _s="sXC0", _n=16)
    P.oi("gpsimd", "dma_start",
         out=wout_bf[:].rearrange("p (c d) -> p c d", c=8),
         in_=d_wout.rearrange("(c p) d -> p c d", p=128), _s="sWO", _n=16)
    P.oi("gpsimd", "dma_start", out=eye_sb[:], in_=d_eye, _s="sWO", _n=16)
    P.oi("gpsimd", "dma_start", out=d_wqbf, in_=d_wqkv[:, 1536:3072],
         _s="sWQ2", _n=16)

    P.o("vector", "memset", ones_bf[:], 1.0)
    P.o("vector", "memset", ones_f[:], 1.0)
    P.o("vector", "memset",
        v_tm[:].rearrange("p (t c) -> p t c", c=80)[:, :, 64:65], 1.0)
    P.w("vector", "sMS", 16 * 8)
    P.o("vector", "memset", brow_o[:], 0.0)
    P.o("vector", "memset", brow_2[:], 0.0)
    P.o("vector", "memset", brow_v[:], 0.0)
    P.d("vector")
    P.o("vector", "tensor_copy", brow_o[0:1, :], brow_f[0:1, :])
    P.d("vector")
    P.o("vector", "tensor_tensor", brow_o[64:65, :], brow_f[0:1, :],
        brow_o[0:1, :], op=ALU.subtract)
    P.o("vector", "tensor_scalar", brow_2[0:1, :], brow_f[32:33, :],
        0.5, None, op0=ALU.mult)
    P.o("vector", "tensor_scalar", x_tm2[0][0:1, :], brow_f[32:33, :],
        0.5, None, op0=ALU.mult)
    P.d("vector")
    P.o("vector", "tensor_tensor", brow_2[64:65, :], x_tm2[0][0:1, :],
        brow_2[0:1, :], op=ALU.subtract)
    P.o("vector", "tensor_copy", brow_v[0:1, :], brow_f[64:65, :])
    P.o("vector", "tensor_copy", x_tm2[1][0:1, :], brow_f[64:65, :])
    P.d("vector")
    vset = P.oi("vector", "tensor_tensor", brow_v[64:65, :], x_tm2[1][0:1, :],
                brow_v[0:1, :], op=ALU.subtract, _s="sV")

    dq_jobs = []
    for g in range(8):
        for cs in range(8):
            dq_jobs.append((d_q1[g * 128:(g + 1) * 128, cs * 512:(cs + 1) * 512],
                            d_s1[g * 128:(g + 1) * 128, cs * 512:(cs + 1) * 512],
                            d_w1s[g * 128:(g + 1) * 128, cs * 512:(cs + 1) * 512],
                            1.0, -128.0))
    for g in range(32):
        for cs in range(2):
            dq_jobs.append((d_q2[g * 128:(g + 1) * 128, cs * 512:(cs + 1) * 512],
                            d_s2[g * 128:(g + 1) * 128, cs * 512:(cs + 1) * 512],
                            d_w2s[g * 128:(g + 1) * 128, cs * 512:(cs + 1) * 512],
                            0.5, -64.0))
    DQ_TOTAL = len(dq_jobs)
    dq_i = [0]
    sc_slot = {0: 0, 1: 0}
    dq_ld = {0: 0, 1: 0}
    dq_wait = {0: 0, 1: 0}

    def dq_issue_loads(i):
        qa, sa, _, _, _ = dq_jobs[i]
        b = i % 2
        if i >= 2:
            P.w("gpsimd", f"sSC{b}", sc_slot[b])
        P.oi("gpsimd", "dma_start", out=dqq[b][:], in_=qa, _s=f"sDQ{b}", _n=16)
        dq_ld[b] = P.oi("gpsimd", "dma_start", out=dqs[b][:], in_=sa,
                        _s=f"sDQ{b}", _n=16)

    def pump_dequant(n):
        for _ in range(n):
            i = dq_i[0]
            if i >= DQ_TOTAL:
                return
            dq_i[0] += 1
            b = i % 2
            if i == 0:
                dq_issue_loads(0)
            wait_val = dq_ld[b]
            if i + 1 < DQ_TOTAL:
                dq_issue_loads(i + 1)
            _, _, wa, mul, off = dq_jobs[i]
            P.w("vector", f"sDQ{b}", wait_val)
            if mul == 1.0:
                gv = P.oi("vector", "scalar_tensor_tensor", dqw[b][:], dqq[b][:],
                          off, dqs[b][:], op0=ALU.add, op1=ALU.mult, _s="sG")
            else:
                P.o("vector", "tensor_scalar", dqt[b][:], dqq[b][:], mul, off,
                    op0=ALU.mult, op1=ALU.add)
                P.d("vector")
                gv = P.oi("vector", "tensor_mul", dqw[b][:], dqt[b][:], dqs[b][:],
                          _s="sG")
            P.w("gpsimd", "sG", gv)
            sc_slot[b] = P.oi("gpsimd", "dma_start", out=wa, in_=dqw[b][:],
                              _s=f"sSC{b}", _n=16)

    wq_pre[0] = P.val("sWQ")
    for e in ENGINES:
        P.w(e, "sMS", 16 * 8)
        P.w(e, "sWO", P.val("sWO"))
        P.w(e, "sV", vset)

    for b in range(NB):
        x_base = b * S
        vone = P.oi("vector", "memset",
                    v_tm[:].rearrange("p (t c) -> p t c", c=80)[:, :, 64:65], 1.0,
                    _s="sV")
        P.w("sync", "sV", vone)
        vcount = 0
        for half in range(2):
            if half == 0:
                wq_ready = wq_pre[0]
            else:
                P.w("gpsimd", "sT", P.val("sT"))
                P.w("gpsimd", "sWQ2", 16)
                for kc in range(8):
                    P.oi("gpsimd", "dma_start",
                         out=wqh[:, kc * 1536:(kc + 1) * 1536],
                         in_=d_wqbf[kc * 128:(kc + 1) * 128, :], _s="sWQ", _n=16)
                wq_ready = P.val("sWQ")
            if b == 0 and half == 0:
                P.oi("gpsimd", "dma_start", out=d_xbfT[:, S:2 * S],
                     in_=d_xT[:, S:2 * S], _s="sXC1", _n=16)
            def emit_x_transposes(t):
                xslot = t % 2
                sx = f"sTPx{xslot}"
                P.w("sync", "sT", xfm_cons[xslot])
                if half == 0 and t == 0:
                    P.w("sync", f"sXC{b}", 16)
                for c in range(8):
                    P.oi("sync", "dma_start",
                         out=x_fm[xslot][:, c * 512:(c + 1) * 512],
                         in_=d_xbfT[c * 128:(c + 1) * 128,
                                    x_base + t * 512: x_base + t * 512 + 512],
                         _s=sx, _n=16)
                xfm_ready[xslot] = P.val(sx)

            for t in range(NG):
                xslot = t % 2
                xf = x_fm[xslot]
                sx = f"sTPx{xslot}"
                if t == 0 and not (b == 1 and half == 0 and xpre[0]):
                    emit_x_transposes(0)
                if b == 1 and half == 0 and t == 0:
                    xpre[0] = 0
                if t + 1 < NG:
                    emit_x_transposes(t + 1)
                pump_dequant(3)
                P.w("tensor", sx, xfm_ready[xslot])
                if t == 0:
                    P.w("tensor", "sWQ", wq_ready)
                for mi in range(12 if half == 0 else 4):
                    m = half * 12 + mi
                    slot = (half * 48 + t * 12 + mi) % 4
                    ps = PS_B[:, slot * 512:(slot + 1) * 512]
                    sem, val = psBq_free[slot]
                    P.w("tensor", sem, val)
                    for kc in range(8):
                        rec = P.o("tensor", "matmul", ps,
                                  wqh[:, kc * 1536 + mi * 128: kc * 1536 + mi * 128 + 128],
                                  xf[:, kc * 512:(kc + 1) * 512],
                                  start=(kc == 0), stop=(kc == 7))
                    mmv = P.inc(rec, "sT")
                    if mi == 11 or (half == 1 and mi == 3):
                        pass
                    P.w("vector", "sT", mmv)
                    bias = bqkv_c[:, m:m + 1]
                    tcol = t * 512
                    if m < 8:
                        ev = P.oi("vector", "tensor_scalar",
                                  q_fm[:, m * S + tcol: m * S + tcol + 512], ps,
                                  bias, None, op0=ALU.add, _s="sV")
                    else:
                        ev = P.oi("vector", "scalar_tensor_tensor",
                                  k_fm[:, (m - 8) * S + tcol: (m - 8) * S + tcol + 512],
                                  ps, bias, pe2[:, tcol:tcol + 512],
                                  op0=ALU.add, op1=ALU.add, _s="sV")
                    psBq_free[slot] = ("sV", ev)
                if half == 1:
                    for sub in range(4):
                        ktile = t * 4 + sub
                        for vh in range(2):
                            vslot = (t * 8 + sub * 2 + vh) % 4
                            psv = PS_A[:, vslot * 512:(vslot + 1) * 512]
                            sem, val = psA_vfree[vslot]
                            P.w("tensor", sem, val)
                            for kc in range(8):
                                P.o("tensor", "matmul", psv,
                                    xf[:, kc * 512 + sub * 128: kc * 512 + sub * 128 + 128],
                                    wqh[:, kc * 1536 + 512 + vh * 512:
                                         kc * 1536 + 512 + vh * 512 + 512],
                                    start=(kc == 0), stop=False)
                            rec = P.o("tensor", "matmul", psv,
                                      ones_bf[0:65, 0:128],
                                      brow_v[0:65, vh * 512: vh * 512 + 512],
                                      start=False, stop=True)
                            mmv = P.inc(rec, "sT")
                            P.w("vector", "sT", mmv)
                            dst = v_tm[:, ktile * 1280 + vh * 8 * 80:
                                       ktile * 1280 + vh * 8 * 80 + 640]
                            ev = P.oi("vector", "tensor_copy",
                                      dst.rearrange("p (h d) -> p h d", d=80)[:, :, 0:64],
                                      psv.rearrange("p (h d) -> p h d", d=64),
                                      _s="sV")
                            psA_vfree[vslot] = ("sV", ev)
                if half == 1 and t == NG - 1:
                    xfm_cons[xslot] = P.val("sT")
                else:
                    xfm_cons[xslot] = P.val("sT")
        def norm_chunk(buf, c, ncol):
            sqb = sq_sb[ncol % 2]
            P.w("scalar", "sTPx0", sq_guard[0])
            P.w("scalar", "sTPx1", sq_guard[1])
            P.w("scalar", "sT", norm_mm_done[ncol % 2])
            sv = P.oi("scalar", "activation", sqb[:, :], buf[:, c * S:(c + 1) * S],
                      AF.Square, bias=f32(0.0), scale=f32(1.0), _s="sS")
            P.w("tensor", "sS", sv)
            for sl in range(4):
                ps = PS_A[0:1, sl * 512: sl * 512 + 512]
                sem, val = psA_norm_free[sl]
                P.w("tensor", sem, val)
                if sl == 3:
                    mmv = P.oi("tensor", "matmul", ps, ones_bf[:, 0:1],
                               sqb[:, sl * 512:(sl + 1) * 512],
                               start=True, stop=True, _s="sT")
                else:
                    rec = P.o("tensor", "matmul", ps, ones_bf[:, 0:1],
                              sqb[:, sl * 512:(sl + 1) * 512],
                              start=True, stop=True)
                    mmv = P.inc(rec, "sT")
                P.w("vector", "sT", mmv)
                rv = P.oi("vector", "reduce_max", nmax[0:1, 4 * ncol + sl: 4 * ncol + sl + 1],
                          ps, axis=AX.X, _s="sV")
                psA_norm_free[sl] = ("sV", rv)
            norm_mm_done[ncol % 2] = P.val("sT")

        sq_guard = {0: 0, 1: 0}
        norm_mm_done = {0: 0, 1: 0}
        psA_norm_free = list(psA_vfree)
        P.w("scalar", "sV", P.val("sV"))
        ncol = 0
        for buf in (q_fm, k_fm):
            for c in range(8):
                norm_chunk(buf, c, ncol)
                ncol += 1
        psA_free[0] = ("sV", P.val("sV"))
        psA_free[1] = ("sV", P.val("sV"))
        P.d("vector")
        P.o("vector", "reduce_max", nmax[0:1, 64:65], nmax[0:1, 0:64], axis=AX.X)
        P.d("vector")
        cv = P.oi("vector", "tensor_scalar", nmaxbf[0:1, 0:1], nmax[0:1, 64:65],
                  -0.125, None, op0=ALU.mult, _s="sV")
        P.w("tensor", "sV", cv)
        sem, val = psBq_free[0]
        P.w("tensor", sem, val)
        mmv = P.oi("tensor", "matmul", PS_B[:, 0:1], ones_bf[0:1, 0:128],
                   nmaxbf[0:1, 0:1], start=True, stop=True, _s="sT")
        P.w("vector", "sT", mmv)
        cbv = P.oi("vector", "tensor_copy", cneg[:], PS_B[:, 0:1], _s="sV")
        psBq_free[0] = ("sV", cbv)
        pump_dequant(8)
        pfx = "dbg_" if b == 0 else "dbg2_"
        dbg_dump([(pfx + "q", q_fm[:]), (pfx + "k", k_fm[:]), (pfx + "v", v_tm[:]),
                  (pfx + "cneg", cneg[:])])

        P.barrier()

        prev = [None]

        def emit_onorm_tensor():
            if prev[0] is None:
                return
            o_half_p, _, _, rv_p, bmv_hold = prev[0]
            P.w("tensor", "sV", rv_p)
            P.o("tensor", "matmul", o_half_p[64:128, 0:512], ones_f[64:65, 0:64],
                recb[64:65, 0:512], start=True, stop=True, tile_position=(64, 64))
            bmv_hold[0] = P.oi("tensor", "matmul", o_half_p[64:128, 512:1024],
                               ones_f[64:65, 0:64], recb[64:65, 512:1024],
                               start=True, stop=True, tile_position=(64, 64), _s="sT")

        def emit_onorm_vector():
            if prev[0] is None:
                return
            o_half_p, ocol_p, par_p, _, bmv_hold = prev[0]
            prev[0] = None
            P.w("vector", "sT", bmv_hold[0])
            P.o("vector", "tensor_copy", bc_sb[0:64, 0:512], o_half_p[64:128, 0:512])
            P.o("vector", "tensor_copy", bc_sb[64:128, 0:512],
                o_half_p[64:128, 512:1024])
            P.d("vector")
            P.o("vector", "tensor_tensor", O_all[0:64, ocol_p:ocol_p + 512],
                o_half_p[0:64, 0:512], bc_sb[0:64, 0:512], op=ALU.mult)
            ov = P.oi("vector", "tensor_tensor", O_all[64:128, ocol_p:ocol_p + 512],
                      o_half_p[0:64, 512:1024], bc_sb[64:128, 0:512],
                      op=ALU.mult, _s="sV")
            psB_half[par_p] = ("sV", ov)

        for g in range(NG):
            qcol = g * 512
            for p in range(PAIRS):
                par = p % 2
                o_half = PS_B[:, 0:1024] if par == 0 else PS_B[:, 1024:2048]
                exp_vals = []
                pv_done = 0

                def s_pair(kt):
                    s_ps = PS_A[:, 0:1024] if kt % 2 == 0 else PS_A[:, 1024:2048]
                    if kt >= 2:
                        P.w("tensor", "sS", exp_vals[kt - 2][1])
                    elif kt == 0:
                        P.w("tensor", psA_free[0][0], psA_free[0][1])
                        P.w("tensor", psA_free[1][0], psA_free[1][1])
                    rec = P.o("tensor", "matmul", s_ps[:, 0:512],
                              k_fm[0:64, p * S + kt * 128: p * S + kt * 128 + 128],
                              q_fm[0:64, p * S + qcol: p * S + qcol + 512],
                              start=True, stop=True, tile_position=(0, 0))
                    sv1 = P.inc(rec, "sT")
                    rec = P.o("tensor", "matmul", s_ps[:, 512:1024],
                              k_fm[64:128, p * S + kt * 128: p * S + kt * 128 + 128],
                              q_fm[64:128, p * S + qcol: p * S + qcol + 512],
                              start=True, stop=True, tile_position=(64, 0))
                    sv2 = P.inc(rec, "sT")
                    P.w("scalar", "sT", sv1)
                    ea = P.oi("scalar", "activation", pT[kt % 2][:, 0:512],
                              s_ps[:, 0:512], AF.Exp, bias=cneg[:, :],
                              scale=f32(0.125), _s="sS")
                    P.w("scalar", "sT", sv2)
                    eb = P.oi("scalar", "activation", pT[kt % 2][:, 512:1024],
                              s_ps[:, 512:1024], AF.Exp, bias=cneg[:, :],
                              scale=f32(0.125), _s="sS")
                    exp_vals.append((ea, eb))

                s_pair(0)
                for kt in range(KT):
                    if kt + 1 < KT:
                        s_pair(kt + 1)
                    if kt == 6:
                        emit_onorm_tensor()
                    P.w("tensor", "sS", exp_vals[kt][0])
                    if kt == 0:
                        P.w("tensor", psB_half[par][0], psB_half[par][1])
                    P.o("tensor", "matmul", o_half[0:65, 0:512],
                        v_tm[:, kt * 1280 + 2 * p * 80: kt * 1280 + 2 * p * 80 + 65],
                        pT[kt % 2][:, 0:512], start=(kt == 0), stop=(kt == KT - 1))
                    P.w("tensor", "sS", exp_vals[kt][1])
                    rec = P.o("tensor", "matmul", o_half[0:65, 512:1024],
                              v_tm[:, kt * 1280 + (2 * p + 1) * 80:
                                   kt * 1280 + (2 * p + 1) * 80 + 65],
                              pT[kt % 2][:, 512:1024],
                              start=(kt == 0), stop=(kt == KT - 1))
                    if kt == KT - 1:
                        pv_done = P.inc(rec, "sT")
                psA_free[0] = ("sS", exp_vals[KT - 2][1])
                psA_free[1] = ("sS", exp_vals[KT - 1][1])
                emit_onorm_vector()
                P.w("vector", "sT", pv_done)
                P.o("vector", "reciprocal", recb[64:65, 0:512], o_half[64:65, 0:512])
                rv = P.oi("vector", "reciprocal", recb[64:65, 512:1024],
                          o_half[64:65, 512:1024], _s="sV")
                prev[0] = (o_half, p * S + qcol, par, rv, [0])
                pump_dequant(3)
        emit_onorm_tensor()
        emit_onorm_vector()
        dbg_dump([(("dbg_" if b == 0 else "dbg2_") + "O", O_all[:])])

        P.barrier()

        fc2_banks = [PS_A[:, 0:1024], PS_A[:, 1024:2048],
                     PS_B[:, 0:1024], PS_B[:, 1024:2048]]
        fc2_free = [("sV", 0)] * 4
        psBq_free = [psB_half[0], psB_half[0], psB_half[1], psB_half[1]]
        fc1_mm = {}
        uv_j = {}
        hv_j = {}
        chain_agv = {}
        chain_ag2 = {}
        oall_done = [0]
        hpe_last = [0]
        t2_war = [0]

        w1ring = w1buf + [w1buf_x]
        w2ring = w2buf + [w2buf_x]

        def xt_prefetch(g2, j):
            row = x_base + g2 * 512 + j * 128
            P.w("sync", "sV", xt_cons[j % 2])
            xt_loads[j % 2] = P.oi("sync", "dma_start", out=x_tm2[j % 2][:],
                                   in_=x_rows[row:row + 128, :],
                                   _s=f"sXT{j % 2}", _n=16)

        def partA_tv(g2, j):
            ps_u = PS_A[:, (j % 2) * 1024:(j % 2) * 1024 + 1024]
            x1j = x1buf[g2 % 2][:, j * DM:(j + 1) * DM]
            P.w("tensor", psA_free[j % 2][0], psA_free[j % 2][1])
            for hf in range(2):
                for p in range(8):
                    P.o("tensor", "matmul", ps_u[:, hf * 512:(hf + 1) * 512],
                        O_all[:, p * S + g2 * 512 + j * 128:
                              p * S + g2 * 512 + j * 128 + 128],
                        wout_bf[:, p * 1024 + hf * 512: p * 1024 + (hf + 1) * 512],
                        start=(p == 0), stop=False)
                rec = P.o("tensor", "matmul", ps_u[:, hf * 512:(hf + 1) * 512],
                          ones_bf[0:65, 0:128],
                          brow_o[0:65, hf * 512: hf * 512 + 512],
                          start=False, stop=True)
            opv = P.inc(rec, "sT")
            if g2 == 3 and j == 3:
                oall_done[0] = opv
            P.w("vector", "sT", opv)
            P.w("vector", f"sXT{j % 2}", xt_loads[j % 2])
            uv = P.oi("vector", "tensor_tensor", x1j, ps_u, x_tm2[j % 2][:],
                      op=ALU.add, _s="sV")
            uv_j[j] = uv
            psA_free[j % 2] = ("sV", uv)
            xt_cons[j % 2] = uv
            if j + 2 < 4:
                xt_prefetch(g2, j + 2)
            P.d("vector")
            P.o("vector", "bn_stats", stats[:, 0:6], x1j[:, 0:512])
            P.o("vector", "bn_stats", stats[:, 6:12], x1j[:, 512:1024])
            P.d("vector")
            chain_agv[j] = P.oi("vector", "bn_aggr", mvp[:, 3 * j:3 * j + 2],
                                stats[:, 0:12], _s="sV")

        def batch_scalarA():
            P.w("scalar", "sV", chain_agv[3])
            for j in range(4):
                P.o("scalar", "activation", mvp[:, 3 * j + 2:3 * j + 3],
                    mvp[:, 3 * j + 1:3 * j + 2], AF.Ln, bias=f32(EPS), scale=f32(1.0))
                P.o("scalar", "activation", mvp[:, 12 + 3 * j + 2:12 + 3 * j + 3],
                    mvp[:, 3 * j + 1:3 * j + 2], AF.Ln, bias=f32(EPS * EPS),
                    scale=f32(1.0 + EPS))
            P.d("scalar")
            for j in range(4):
                P.o("scalar", "activation", mvp[:, 3 * j + 2:3 * j + 3],
                    mvp[:, 3 * j + 2:3 * j + 3], AF.Exp, bias=f32(0.0),
                    scale=f32(-0.5))
                rec = P.o("scalar", "activation", mvp[:, 12 + 3 * j + 2:12 + 3 * j + 3],
                          mvp[:, 12 + 3 * j + 2:12 + 3 * j + 3], AF.Exp, bias=f32(0.0),
                          scale=f32(-0.5))
            mvp_sc[0] = P.inc(rec, "sS")

        def applyAB(g2, j):
            x1j = x1buf[g2 % 2][:, j * DM:(j + 1) * DM]
            c0 = 3 * j
            c1 = 12 + 3 * j
            P.w("vector", "sS", mvp_sc[0])
            P.o("vector", "tensor_scalar", t2_sb[:], x1j, mvp[:, c0:c0 + 1],
                mvp[:, c1 + 2:c1 + 3], op0=ALU.subtract, op1=ALU.mult)
            P.d("vector")
            P.o("vector", "tensor_scalar", x1j, x1j, mvp[:, c0:c0 + 1],
                mvp[:, c0 + 2:c0 + 3], op0=ALU.subtract, op1=ALU.mult)
            P.o("vector", "tensor_tensor", u_sb[:], t2_sb[:], gam_bc[:], op=ALU.mult)
            P.d("vector")
            P.w("vector", htm_cons[j][0], htm_cons[j][1])
            hv_j[j] = P.oi("vector", "tensor_tensor", h_tm[j][:], u_sb[:], bet_bc[:],
                           op=ALU.add, _s="sV")

        def h_transposes_pe(j):
            half = PS_A[:, (j % 2) * 1024:(j % 2) * 1024 + 1024]
            P.w("tensor", "sV", hv_j[j])
            P.w("tensor", psA_free[j % 2][0], psA_free[j % 2][1])
            for c in range(8):
                rec = P.o("tensor", "matmul", half[:, c * 128:(c + 1) * 128],
                          h_tm[j][:, c * 128:(c + 1) * 128], eye_sb[:],
                          start=True, stop=True)
            mmv = P.inc(rec, "sT")
            htm_cons[j] = ("sT", mmv)
            P.w("scalar", "sT", mmv)
            for c in range(8):
                rec2 = P.o("scalar", "activation",
                           h_fm[:, c * 512 + j * 128: c * 512 + j * 128 + 128],
                           half[:, c * 128:(c + 1) * 128], AF.Copy,
                           bias=0.0, scale=1.0)
            ev = P.inc(rec2, "sS")
            psA_free[j % 2] = ("sS", ev)
            hpe_last[0] = ev

        def h_transposes(g2, j, also_wait_fc1=0):
            P.w("sync", "sV", hv_j[j])
            if also_wait_fc1:
                P.w("sync", "sT", also_wait_fc1)
            for c in range(8):
                P.oi("sync", "dma_start",
                     out=h_fm[:, c * 512 + j * 128: c * 512 + j * 128 + 128],
                     in_=h_tm[j][:, c * 128:(c + 1) * 128], transpose=True,
                     _s="sTPh", _n=16)
            htm_cons[j] = ("sTPh", P.val("sTPh"))

        partA_tv_at = {3: 0, 7: 1, 11: 2, 15: 3}
        for g in range(NG):
            if g == 0:
                xt_prefetch(0, 0)
                xt_prefetch(0, 1)
                for j in range(4):
                    partA_tv(0, j)
                batch_scalarA()
                for j in range(4):
                    applyAB(0, j)
                    h_transposes_pe(j)
            if g == 0:
                P.w("tensor", "sS", hpe_last[0])
            else:
                P.w("tensor", "sTPh", P.val("sTPh"))
            if b == 0 and g == 0:
                pump_dequant(DQ_TOTAL)
                assert dq_i[0] == DQ_TOTAL
                P.w("gpsimd", "sSC0", sc_slot[0])
                P.w("gpsimd", "sSC1", sc_slot[1])
            silu_vals = {}
            for f in range(32):
                wb3 = f % 3
                sw = "sW1" + "abc"[wb3]
                P.w("gpsimd", "sT", w1_cons[wb3])
                wv1 = P.oi("gpsimd", "dma_start",
                           out=w1ring[wb3][:].rearrange("p (k m) -> p k m", k=8),
                           in_=d_w1s[:, f * 128:(f + 1) * 128].rearrange(
                               "(k p) m -> p k m", p=128), _s=sw, _n=16)
                P.w("tensor", sw, wv1)
                ps = PS_B[:, (f % 4) * 512:(f % 4) * 512 + 512]
                if f >= 4:
                    P.w("tensor", "sS", silu_vals[f - 4])
                else:
                    P.w("tensor", psBq_free[f % 4][0], psBq_free[f % 4][1])
                for kc in range(8):
                    rec = P.o("tensor", "matmul", ps,
                              w1ring[wb3][:, kc * 128:(kc + 1) * 128],
                              h_fm[:, kc * 512:(kc + 1) * 512],
                              start=(kc == 0), stop=(kc == 7))
                fc1_mm[f] = w1_cons[wb3] = P.inc(rec, "sT")
                P.w("scalar", "sT", fc1_mm[f])
                silu_vals[f] = P.oi("scalar", "activation",
                                    y_g[:, f * 512:(f + 1) * 512], ps, AF.Silu,
                                    bias=b1_c[:, f:f + 1], scale=f32(1.0), _s="sS")
                if g + 1 < NG:
                    if f == 1:
                        xt_prefetch(g + 1, 0)
                        xt_prefetch(g + 1, 1)
                    if f in partA_tv_at:
                        partA_tv(g + 1, partA_tv_at[f])
                    if f == 17:
                        batch_scalarA()
                    if f in (19, 22, 25, 28):
                        applyAB(g + 1, (f - 19) // 3)
            for sl in range(4):
                psBq_free[sl] = ("sS", silu_vals[28 + sl])
            for f in range(32):
                wb3 = f % 3
                sw = "sW2" + "abc"[wb3]
                P.w("gpsimd", "sT", w2_cons[wb3])
                wv2 = P.oi("gpsimd", "dma_start", out=w2ring[wb3][:],
                           in_=d_w2s[f * 128:(f + 1) * 128, :], _s=sw, _n=16)
                P.w("tensor", sw, wv2)
                if f == 0:
                    P.w("tensor", "sS", silu_vals[31])
                    if g + 1 < NG:
                        P.w("tensor", "sV", uv_j[2])
                        P.w("tensor", "sV", uv_j[3])
                    for jj in range(4):
                        P.w("tensor", fc2_free[jj][0], fc2_free[jj][1])
                for j in range(4):
                    for hf in range(2):
                        rec = P.o("tensor", "matmul",
                                  fc2_banks[j][:, hf * 512:(hf + 1) * 512],
                                  y_g[:, f * 512 + j * 128: f * 512 + j * 128 + 128],
                                  w2ring[wb3][:, hf * 512:(hf + 1) * 512],
                                  start=(f == 0), stop=False)
                w2_cons[wb3] = P.inc(rec, "sT")
                if g + 1 < NG and f in (1, 4, 7, 10):
                    h_transposes(g + 1, (f - 1) // 3,
                                 also_wait_fc1=fc1_mm[31] if f == 1 else 0)
            for j in range(4):
                for hf in range(2):
                    rec = P.o("tensor", "matmul",
                              fc2_banks[j][:, hf * 512:(hf + 1) * 512],
                              ones_bf[0:65, 0:128],
                              brow_2[0:65, hf * 512: hf * 512 + 512],
                              start=False, stop=True)
            z_done = P.inc(rec, "sT")
            for j in (2, 3, 0, 1):
                ob = j % 2
                P.w("vector", "sT", z_done)
                P.w("vector", f"sOUT{ob}", out_cons[ob])
                x1j = x1buf[g % 2][:, j * DM:(j + 1) * DM]
                P.w("vector", "sV", t2_war[0])
                tv = P.oi("vector", "tensor_tensor", t2_sb[:], fc2_banks[j], x1j,
                          op=ALU.add, _s="sV")
                fc2_free[j] = ("sV", tv)
                P.d("vector")
                P.o("vector", "bn_stats", stats3[:, 0:6], t2_sb[:, 0:512])
                P.o("vector", "bn_stats", stats3[:, 6:12], t2_sb[:, 512:1024])
                P.d("vector")
                P.w("vector", "sS", mvp_sc[2])
                agv = P.oi("vector", "bn_aggr", mvp2[:, 0:2], stats3[:, 0:12], _s="sV")
                P.w("scalar", "sV", agv)
                P.o("scalar", "activation", mvp2[:, 2:3], mvp2[:, 1:2], AF.Ln,
                    bias=f32(EPS), scale=f32(1.0))
                P.d("scalar")
                mvp_sc[2] = P.oi("scalar", "activation", mvp2[:, 2:3], mvp2[:, 2:3],
                                 AF.Exp, bias=f32(0.0), scale=f32(-0.5), _s="sS")
                P.w("vector", "sS", mvp_sc[2])
                wv = P.oi("vector", "tensor_scalar", out_sb[ob][:], t2_sb[:],
                          mvp2[:, 0:1], mvp2[:, 2:3], op0=ALU.subtract, op1=ALU.mult,
                          _s="sV")
                t2_war[0] = wv
                P.w("sync", "sV", wv)
                row = x_base + g * 512 + j * 128
                out_cons[ob] = P.oi("sync", "dma_start", out=y_rows[row:row + 128, :],
                                    in_=out_sb[ob][:], _s=f"sOUT{ob}", _n=16)
            psA_free[0] = fc2_free[0]
            psA_free[1] = fc2_free[1]
            psBq_free[0] = psBq_free[1] = fc2_free[2]
            psBq_free[2] = psBq_free[3] = fc2_free[3]
            if g == 0:
                pfx2 = "dbg_" if b == 0 else "dbg2_"
                dbg_dump([(pfx2 + "x1", x1buf[0][:]), (pfx2 + "h", h_fm[:]),
                          (pfx2 + "y", y_g[:])])
        psB_half = {0: fc2_free[2], 1: fc2_free[3]}
        if b == 0:
            P.w("gpsimd", "sT", oall_done[0])
            for kc in range(8):
                P.oi("gpsimd", "dma_start",
                     out=wqh[:, kc * 1536:(kc + 1) * 1536],
                     in_=d_wqkv[kc * 128:(kc + 1) * 128, 0:1536], _s="sWQ", _n=16)
            wq_pre[0] = P.val("sWQ")
            P.w("sync", "sT", max(oall_done[0], xfm_cons[0]))
            P.w("sync", "sXC1", 16)
            for c in range(8):
                P.oi("sync", "dma_start",
                     out=x_fm[0][:, c * 512:(c + 1) * 512],
                     in_=d_xbfT[c * 128:(c + 1) * 128, S: S + 512],
                     _s="sTPx0", _n=16)
            xfm_ready[0] = P.val("sTPx0")
            xpre[0] = 1
        P.barrier()

    assert dq_i[0] == DQ_TOTAL
    assert debug or P.val("sOUT0") + P.val("sOUT1") == 16 * NB * 16
    P.w("sync", "sOUT0", P.val("sOUT0"))
    P.w("sync", "sOUT1", P.val("sOUT1"))

    def run_ops(e, eng_name):
        for rec in P.ops[eng_name]:
            if rec[0] == "w":
                e.wait_ge(sems[rec[1]], rec[2])
            elif rec[0] == "d":
                e.drain()
            else:
                ins = getattr(e, rec[1])(*rec[2], **rec[3])
                if rec[4] is not None:
                    ins.then_inc(sems[rec[4][0]], rec[4][1])

    with es, nc.Block() as block:
        @block.sync
        def _(e):
            run_ops(e, "sync")

        @block.gpsimd
        def _(e):
            run_ops(e, "gpsimd")

        @block.tensor
        def _(e):
            run_ops(e, "tensor")

        @block.vector
        def _(e):
            run_ops(e, "vector")

        @block.scalar
        def _(e):
            run_ops(e, "scalar")

    return nc


def _pos_enc():
    positions = np.arange(S, dtype=np.float32)[:, None]
    div_term = np.exp(np.arange(0, 64, 2, dtype=np.float32)
                      * f32(-(np.log(10000.0) / 64)))
    pe = np.zeros((S, 64), dtype=np.float32)
    pe[:, 0::2] = np.sin(positions * div_term)
    pe[:, 1::2] = np.cos(positions * div_term)
    return pe


_return_in_maps = [False]


def kernel(L0_attn_x, L0_attn_W_qkv, L0_attn_b_qkv, L0_attn_W_out, L0_attn_b_out,
           L0_ffn_ln_gamma, L0_ffn_ln_beta, L0_ffn_fc1_q, L0_ffn_fc1_scales,
           L0_ffn_fc1_bias, L0_ffn_fc2_q, L0_ffn_fc2_scales, L0_ffn_fc2_bias):
    x = np.ascontiguousarray(np.asarray(L0_attn_x, dtype=np.float32))
    wqkv = np.ascontiguousarray(np.asarray(L0_attn_W_qkv, dtype=np.float32))
    bqkv = np.asarray(L0_attn_b_qkv, dtype=np.float32)
    wout = np.ascontiguousarray(np.asarray(L0_attn_W_out, dtype=np.float32))
    bout = np.asarray(L0_attn_b_out, dtype=np.float32)
    gam = np.asarray(L0_ffn_ln_gamma, dtype=np.float32)
    bet = np.asarray(L0_ffn_ln_beta, dtype=np.float32)
    q1 = np.ascontiguousarray(np.asarray(L0_ffn_fc1_q, dtype=np.int32))
    s1 = np.asarray(L0_ffn_fc1_scales, dtype=np.float32)
    b1 = np.asarray(L0_ffn_fc1_bias, dtype=np.float32)
    q2 = np.ascontiguousarray(np.asarray(L0_ffn_fc2_q, dtype=np.int32))
    s2 = np.asarray(L0_ffn_fc2_scales, dtype=np.float32)
    b2 = np.asarray(L0_ffn_fc2_bias, dtype=np.float32)

    pe = _pos_enc()
    shared = dict(
        wqkv=wqkv,
        wout=wout,
        bqkv_c=np.ascontiguousarray(bqkv.reshape(24, 128).T),
        bout_r=np.ascontiguousarray(bout.reshape(1, DM)),
        bv_r=np.ascontiguousarray(bqkv[2048:3072].reshape(1, DM)),
        gam_bc=np.ascontiguousarray(np.broadcast_to(gam[None, :], (128, DM))),
        bet_bc=np.ascontiguousarray(np.broadcast_to(bet[None, :], (128, DM))),
        q1=q1,
        s1bc=np.ascontiguousarray(
            np.broadcast_to(s1[:, None, :], (8, 128, FF)).reshape(DM, FF)),
        b1_c=np.ascontiguousarray(b1.reshape(32, 128).T),
        q2=q2,
        s2bc=np.ascontiguousarray(
            np.broadcast_to(s2[:, None, :], (32, 128, DM)).reshape(FF, DM)),
        b2_r=np.ascontiguousarray(b2.reshape(1, DM)),
        pe2=np.ascontiguousarray(np.concatenate([pe.T, pe.T], axis=0)),
    )
    in_maps = []
    for c in range(8):
        m = dict(shared)
        m["x"] = np.ascontiguousarray(x[c * NB:(c + 1) * NB])
        m["xt"] = np.ascontiguousarray(
            x[c * NB:(c + 1) * NB].reshape(NB * 2048, 1024).T)
        m["eye"] = np.eye(128, dtype=np.float32).astype(ml_dtypes.bfloat16)
        in_maps.append(m)
    if _return_in_maps[0]:
        return in_maps

    nc = build_program()
    res = run_bass_kernel_spmd(nc, in_maps, core_ids=list(range(8)))
    out = np.concatenate([np.asarray(r["y"], dtype=np.float32) for r in res.results],
                         axis=0)
    return out


# version 56 (final)
# speedup vs baseline: 1.1628x; 1.0142x over previous
import numpy as np
import ml_dtypes
import contextlib
import concourse.bass as bass
import concourse.mybir as mybir
from concourse.bass_utils import run_bass_kernel_spmd

F32 = mybir.dt.float32
BF16 = mybir.dt.bfloat16
I32 = mybir.dt.int32
AF = mybir.ActivationFunctionType
ALU = mybir.AluOpType
AX = mybir.AxisListType
f32 = np.float32

NB = 2
S = 2048
DM = 1024
H = 16
FF = 4096
NG = 4
KT = 16
PAIRS = 8
EPS = 1e-5

ENGINES = ("sync", "gpsimd", "tensor", "vector", "scalar")


class Plan:

    def __init__(self):
        self.ops = {e: [] for e in ENGINES}
        self.sem = {}

    def val(self, s):
        return self.sem.get(s, 0)

    def w(self, eng, s, v):
        if v > 0:
            self.ops[eng].append(("w", s, v))

    def d(self, eng):
        self.ops[eng].append(("d",))

    def o(self, eng, method, *args, **kw):
        rec = ["o", method, args, kw, None]
        self.ops[eng].append(rec)
        return rec

    def inc(self, rec, s, n=1):
        assert rec[4] is None
        self.sem[s] = self.sem.get(s, 0) + n
        rec[4] = (s, n)
        return self.sem[s]

    def oi(self, eng, method, *args, _s=None, _n=1, **kw):
        rec = self.o(eng, method, *args, **kw)
        self.inc(rec, _s, _n)
        return self.sem[_s]

    def hinc(self, eng, s):
        self.d(eng)
        m = "engine_nop" if eng in ("vector", "gpsimd") else "nop"
        return self.oi(eng, m, _s=s)

    BARRIER_EXCLUDE = ("sSC0", "sSC1", "sDQ0", "sDQ1", "sG")

    def barrier(self):
        snap = dict(self.sem)
        for e in ENGINES:
            for s, v in snap.items():
                if v > 0 and s not in self.BARRIER_EXCLUDE:
                    self.ops[e].append(("w", s, v))


def build_program(debug=False):
    nc = bass.Bass("TRN2", target_bir_lowering=False)

    d_x = nc.dram_tensor("x", [NB, S, DM], F32, kind="ExternalInput").ap()
    d_xT = nc.dram_tensor("xt", [DM, NB * S], F32, kind="ExternalInput").ap()
    d_eye = nc.dram_tensor("eye", [128, 128], BF16, kind="ExternalInput").ap()
    d_wqbf = nc.dram_tensor("wqbf", [1024, 1536], BF16, kind="Internal").ap()
    d_wqkv = nc.dram_tensor("wqkv", [DM, 3 * DM], F32, kind="ExternalInput").ap()
    d_wout = nc.dram_tensor("wout", [DM, DM], F32, kind="ExternalInput").ap()
    d_bqkv = nc.dram_tensor("bqkv_c", [128, 24], F32, kind="ExternalInput").ap()
    d_bout = nc.dram_tensor("bout_r", [1, DM], F32, kind="ExternalInput").ap()
    d_bv = nc.dram_tensor("bv_r", [1, DM], F32, kind="ExternalInput").ap()
    d_gam = nc.dram_tensor("gam_bc", [128, DM], F32, kind="ExternalInput").ap()
    d_bet = nc.dram_tensor("bet_bc", [128, DM], F32, kind="ExternalInput").ap()
    d_q1 = nc.dram_tensor("q1", [DM, FF], I32, kind="ExternalInput").ap()
    d_s1 = nc.dram_tensor("s1bc", [DM, FF], F32, kind="ExternalInput").ap()
    d_b1 = nc.dram_tensor("b1_c", [128, 32], F32, kind="ExternalInput").ap()
    d_q2 = nc.dram_tensor("q2", [FF, DM], I32, kind="ExternalInput").ap()
    d_s2 = nc.dram_tensor("s2bc", [FF, DM], F32, kind="ExternalInput").ap()
    d_b2 = nc.dram_tensor("b2_r", [1, DM], F32, kind="ExternalInput").ap()
    d_pe2 = nc.dram_tensor("pe2", [128, S], F32, kind="ExternalInput").ap()
    d_y = nc.dram_tensor("y", [NB, S, DM], F32, kind="ExternalOutput").ap()
    d_w1s = nc.dram_tensor("w1s", [DM, FF], BF16, kind="Internal").ap()
    d_w2s = nc.dram_tensor("w2s", [FF, DM], BF16, kind="Internal").ap()
    d_xbfT = nc.dram_tensor("xbft", [DM, NB * S], BF16, kind="Internal").ap()

    x_rows = d_x.rearrange("b s d -> (b s) d")
    y_rows = d_y.rearrange("b s d -> (b s) d")

    dbg_t = {}
    if debug:
        for nm, shp, dt in (("dbg_q", [128, 8 * S], BF16), ("dbg_k", [128, 8 * S], BF16),
                            ("dbg_v", [128, KT * H * 80], BF16), ("dbg_cneg", [128, 1], F32),
                            ("dbg_O", [128, 8 * S], BF16), ("dbg_x1", [128, 4 * DM], F32),
                            ("dbg_h", [128, 8 * 512], BF16), ("dbg_y", [128, 32 * 512], BF16),
                            ("dbg2_q", [128, 8 * S], BF16), ("dbg2_k", [128, 8 * S], BF16),
                            ("dbg2_v", [128, KT * H * 80], BF16), ("dbg2_cneg", [128, 1], F32),
                            ("dbg2_O", [128, 8 * S], BF16), ("dbg2_x1", [128, 4 * DM], F32),
                            ("dbg2_h", [128, 8 * 512], BF16), ("dbg2_y", [128, 32 * 512], BF16)):
            dbg_t[nm] = nc.dram_tensor(nm, shp, dt, kind="ExternalOutput").ap()

    SB_BASE = 16512
    SBUF_LIMIT = 229344
    alloc = nc.alloc_sbuf_tensor_at
    cur = [SB_BASE]

    def place(name, shape, dt, at=None):
        if at is None:
            at = cur[0]
        nbytes = int(np.prod(shape[1:])) * mybir.dt.size(dt)
        t = alloc(name, shape, dt, offset=at)
        end = (at + nbytes + 31) // 32 * 32
        cur[0] = max(cur[0], end)
        assert cur[0] <= SBUF_LIMIT, (name, cur[0])
        return t

    ones_bf = place("ones_bf", [128, 128], BF16)
    ones_f = place("ones_f", [128, 64], F32)
    bqkv_c = place("bqkv_c", [128, 24], F32)
    b1_c = place("b1_c", [128, 32], F32)
    cneg = place("cneg", [128, 1], F32)
    nmax = place("nmax", [1, 72], F32)
    nmaxbf = place("nmaxbf", [1, 8], BF16)
    stats = place("stats", [128, 12], F32)
    stats2 = place("stats2", [128, 12], F32)
    stats3 = place("stats3", [128, 12], F32)
    mvp = place("mvp", [128, 24], F32)
    mvp2 = place("mvp2", [128, 4], F32)
    brow_o = place("brow_o", [128, 1024], BF16)
    brow_2 = place("brow_2", [128, 1024], BF16)
    brow_v = place("brow_v", [128, 1024], BF16)
    w1buf_x = place("w1buf2", [128, 1024], BF16)
    gam_bc = place("gam_bc", [128, DM], F32)
    bet_bc = place("bet_bc", [128, DM], F32)
    wout_bf = place("wout_bf", [128, 8 * DM], BF16)
    pe2 = place("pe2", [128, S], F32)
    assert cur[0] <= SB_BASE + 43008, cur[0]

    B0 = SB_BASE + 43008
    q_fm = place("q_fm", [128, 8 * S], BF16, at=B0)
    k_fm = place("k_fm", [128, 8 * S], BF16, at=B0 + 32768)
    v_tm = place("v_tm", [128, KT * H * 80], BF16, at=B0 + 65536)
    B_END = B0 + 65536 + 40960
    x_tm2 = [place(f"x_tm2_{i}", [128, DM], F32, at=B0 + i * 4096) for i in range(2)]
    u_sb = place("u_sb", [128, DM], F32, at=B0 + 8192)
    x1buf = [place(f"x1_{i}", [128, 4 * DM], F32, at=B0 + 12288 + i * 16384)
             for i in range(2)]
    t2_sb = place("t2_sb", [128, DM], F32, at=B0 + 45056)
    h_tm = [place(f"h_tm{i}", [128, DM], BF16, at=B0 + 49152 + i * 2048)
            for i in range(4)]
    h_fm = place("h_fm", [128, 8 * 512], BF16, at=B0 + 57344)
    y_g = place("y_g", [128, 32 * 512], BF16, at=B0 + 65536)
    w1buf = [place(f"w1buf{i}", [128, 1024], BF16, at=B0 + 98304 + i * 2048)
             for i in range(2)]
    w2buf = [place(f"w2buf{i}", [128, 1024], BF16, at=B0 + 102400 + i * 2048)
             for i in range(2)]
    assert B0 + 102400 + 4096 <= B_END

    C0 = B_END
    wqh = place("wqh", [128, 8 * 1536], BF16, at=C0)
    x_fm = [place(f"x_fm{i}", [128, 8 * 512], BF16, at=C0 + 24576 + i * 8192)
            for i in range(2)]
    sq_sb = [place(f"sq_sb{i}", [128, S], BF16, at=C0 + 24576 + i * 8192)
             for i in range(2)]
    O_all = place("O_all", [128, 8 * S], BF16, at=C0)
    pT = [place(f"pT{i}", [128, 1024], BF16, at=C0 + 32768 + i * 2048) for i in range(2)]
    recb = place("recb", [128, 1024], F32, at=C0 + 36864)
    bc_sb = place("bc_sb", [128, 1024], F32, at=C0 + 40960)
    C_END = C0 + 45056
    out_sb = [place(f"out_sb{i}", [128, DM], F32, at=C0 + 36864 + i * 4096)
              for i in range(2)]

    D0 = C_END
    dqq = [place(f"dqq{i}", [128, 512], I32, at=D0 + i * 2048) for i in range(2)]
    dqs = [place(f"dqs{i}", [128, 512], F32, at=D0 + 4096 + i * 2048) for i in range(2)]
    dqt = [place(f"dqt{i}", [128, 512], F32, at=D0 + 8192 + i * 2048) for i in range(2)]
    dqw = [place(f"dqw{i}", [128, 512], BF16, at=D0 + 12288 + i * 1024) for i in range(2)]
    brow_f = place("brow_f", [128, 1024], F32, at=C0 + 36864)
    w2buf_x = place("w2buf2", [128, 1024], BF16, at=D0 + 14336)
    eye_sb = place("eye_sb", [128, 128], BF16, at=D0 + 16384)
    assert D0 + 16640 <= SBUF_LIMIT, D0 + 16640

    PS_A = nc.alloc_psum_tensor("PS_A", [128, 2048], F32)
    PS_B = nc.alloc_psum_tensor("PS_B", [128, 2048], F32)

    es = contextlib.ExitStack()
    sems = {}
    for s in ("sXC0", "sXC1", "sWQ", "sWO", "sXT0", "sXT1",
              "sW1a", "sW1b", "sW1c", "sW2a", "sW2b", "sW2c",
              "sDQ0", "sDQ1", "sSC0", "sSC1", "sOUT0", "sOUT1", "sMS",
              "sTPx0", "sTPx1", "sT", "sV", "sS", "sG",
              "sWQ2", "sTPh0", "sTPh1", "sTPh2", "sTPh3"):
        sems[s] = es.enter_context(nc.semaphore(s))

    P = Plan()

    def dbg_dump(pairs):
        if not debug:
            return
        P.barrier()
        for nm, ap in pairs:
            P.oi("sync", "dma_start", out=dbg_t[nm], in_=ap, _s="sOUT0", _n=16)
        P.barrier()

    psA_free = {0: ("sV", 0), 1: ("sV", 0)}
    psB_half = {0: ("sV", 0), 1: ("sV", 0)}
    psBq_free = [("sV", 0)] * 4
    vtmp_cons = {0: 0, 1: 0}
    xfm_cons = {0: 0, 1: 0}
    xfm_ready = {0: 0, 1: 0}
    psA_vfree = [("sV", 0)] * 4
    wq_pre = [0]
    xpre = [0]
    htm_cons = [(f"sTPh{j}", 0) for j in range(4)]
    xt_loads = {0: 0, 1: 0}
    xt_cons = {0: 0, 1: 0}
    out_cons = {0: 0, 1: 0}
    mvp_sc = {0: 0, 1: 0, 2: 0}
    w1_cons = [0] * 4
    w2_cons = [0] * 4

    P.oi("sync", "dma_start", out=bqkv_c[:], in_=d_bqkv, _s="sMS", _n=16)
    P.oi("sync", "dma_start", out=b1_c[:], in_=d_b1, _s="sMS", _n=16)
    P.oi("sync", "dma_start", out=gam_bc[:], in_=d_gam, _s="sMS", _n=16)
    P.oi("sync", "dma_start", out=bet_bc[:], in_=d_bet, _s="sMS", _n=16)
    P.oi("sync", "dma_start", out=pe2[:], in_=d_pe2, _s="sMS", _n=16)
    P.oi("sync", "dma_start", out=brow_f[0:1, :], in_=d_bout, _s="sMS", _n=16)
    P.oi("sync", "dma_start", out=brow_f[32:33, :], in_=d_b2, _s="sMS", _n=16)
    P.oi("sync", "dma_start", out=brow_f[64:65, :], in_=d_bv, _s="sMS", _n=16)

    for kc in range(8):
        P.oi("gpsimd", "dma_start",
             out=wqh[:, kc * 1536:(kc + 1) * 1536],
             in_=d_wqkv[kc * 128:(kc + 1) * 128, 0:1536], _s="sWQ", _n=16)
    P.oi("gpsimd", "dma_start", out=d_xbfT[:, 0:S], in_=d_xT[:, 0:S], _s="sXC0", _n=16)
    P.oi("gpsimd", "dma_start",
         out=wout_bf[:].rearrange("p (c d) -> p c d", c=8),
         in_=d_wout.rearrange("(c p) d -> p c d", p=128), _s="sWO", _n=16)
    P.oi("gpsimd", "dma_start", out=eye_sb[:], in_=d_eye, _s="sWO", _n=16)
    P.oi("gpsimd", "dma_start", out=d_wqbf, in_=d_wqkv[:, 1536:3072],
         _s="sWQ2", _n=16)

    P.o("vector", "memset", ones_bf[:], 1.0)
    P.o("vector", "memset", ones_f[:], 1.0)
    P.o("vector", "memset",
        v_tm[:].rearrange("p (t c) -> p t c", c=80)[:, :, 64:65], 1.0)
    P.w("vector", "sMS", 16 * 8)
    P.o("vector", "memset", brow_o[:], 0.0)
    P.o("vector", "memset", brow_2[:], 0.0)
    P.o("vector", "memset", brow_v[:], 0.0)
    P.d("vector")
    P.o("vector", "tensor_copy", brow_o[0:1, :], brow_f[0:1, :])
    P.d("vector")
    P.o("vector", "tensor_tensor", brow_o[64:65, :], brow_f[0:1, :],
        brow_o[0:1, :], op=ALU.subtract)
    P.o("vector", "tensor_scalar", brow_2[0:1, :], brow_f[32:33, :],
        0.5, None, op0=ALU.mult)
    P.o("vector", "tensor_scalar", x_tm2[0][0:1, :], brow_f[32:33, :],
        0.5, None, op0=ALU.mult)
    P.d("vector")
    P.o("vector", "tensor_tensor", brow_2[64:65, :], x_tm2[0][0:1, :],
        brow_2[0:1, :], op=ALU.subtract)
    P.o("vector", "tensor_copy", brow_v[0:1, :], brow_f[64:65, :])
    P.o("vector", "tensor_copy", x_tm2[1][0:1, :], brow_f[64:65, :])
    P.d("vector")
    vset = P.oi("vector", "tensor_tensor", brow_v[64:65, :], x_tm2[1][0:1, :],
                brow_v[0:1, :], op=ALU.subtract, _s="sV")

    dq_jobs = []
    for g in range(8):
        for cs in range(8):
            dq_jobs.append((d_q1[g * 128:(g + 1) * 128, cs * 512:(cs + 1) * 512],
                            d_s1[g * 128:(g + 1) * 128, cs * 512:(cs + 1) * 512],
                            d_w1s[g * 128:(g + 1) * 128, cs * 512:(cs + 1) * 512],
                            1.0, -128.0))
    for g in range(32):
        for cs in range(2):
            dq_jobs.append((d_q2[g * 128:(g + 1) * 128, cs * 512:(cs + 1) * 512],
                            d_s2[g * 128:(g + 1) * 128, cs * 512:(cs + 1) * 512],
                            d_w2s[g * 128:(g + 1) * 128, cs * 512:(cs + 1) * 512],
                            0.5, -64.0))
    DQ_TOTAL = len(dq_jobs)
    dq_i = [0]
    sc_slot = {0: 0, 1: 0}
    dq_ld = {0: 0, 1: 0}
    dq_wait = {0: 0, 1: 0}

    def dq_issue_loads(i):
        qa, sa, _, _, _ = dq_jobs[i]
        b = i % 2
        if i >= 2:
            P.w("gpsimd", f"sSC{b}", sc_slot[b])
        P.oi("gpsimd", "dma_start", out=dqq[b][:], in_=qa, _s=f"sDQ{b}", _n=16)
        dq_ld[b] = P.oi("gpsimd", "dma_start", out=dqs[b][:], in_=sa,
                        _s=f"sDQ{b}", _n=16)

    def pump_dequant(n):
        for _ in range(n):
            i = dq_i[0]
            if i >= DQ_TOTAL:
                return
            dq_i[0] += 1
            b = i % 2
            if i == 0:
                dq_issue_loads(0)
            wait_val = dq_ld[b]
            if i + 1 < DQ_TOTAL:
                dq_issue_loads(i + 1)
            _, _, wa, mul, off = dq_jobs[i]
            P.w("vector", f"sDQ{b}", wait_val)
            if mul == 1.0:
                gv = P.oi("vector", "scalar_tensor_tensor", dqw[b][:], dqq[b][:],
                          off, dqs[b][:], op0=ALU.add, op1=ALU.mult, _s="sG")
            else:
                P.o("vector", "tensor_scalar", dqt[b][:], dqq[b][:], mul, off,
                    op0=ALU.mult, op1=ALU.add)
                P.d("vector")
                gv = P.oi("vector", "tensor_mul", dqw[b][:], dqt[b][:], dqs[b][:],
                          _s="sG")
            P.w("gpsimd", "sG", gv)
            sc_slot[b] = P.oi("gpsimd", "dma_start", out=wa, in_=dqw[b][:],
                              _s=f"sSC{b}", _n=16)

    wq_pre[0] = P.val("sWQ")
    for e in ENGINES:
        P.w(e, "sMS", 16 * 8)
        P.w(e, "sWO", P.val("sWO"))
        P.w(e, "sV", vset)

    for b in range(NB):
        x_base = b * S
        vone = P.oi("vector", "memset",
                    v_tm[:].rearrange("p (t c) -> p t c", c=80)[:, :, 64:65], 1.0,
                    _s="sV")
        P.w("sync", "sV", vone)
        vcount = 0
        for half in range(2):
            if half == 0:
                wq_ready = wq_pre[0]
            else:
                P.w("gpsimd", "sT", P.val("sT"))
                P.w("gpsimd", "sWQ2", 16)
                for kc in range(8):
                    P.oi("gpsimd", "dma_start",
                         out=wqh[:, kc * 1536:(kc + 1) * 1536],
                         in_=d_wqbf[kc * 128:(kc + 1) * 128, :], _s="sWQ", _n=16)
                wq_ready = P.val("sWQ")
            if b == 0 and half == 0:
                P.oi("gpsimd", "dma_start", out=d_xbfT[:, S:2 * S],
                     in_=d_xT[:, S:2 * S], _s="sXC1", _n=16)
            def emit_x_transposes(t):
                xslot = t % 2
                sx = f"sTPx{xslot}"
                P.w("sync", "sT", xfm_cons[xslot])
                if half == 0 and t == 0:
                    P.w("sync", f"sXC{b}", 16)
                for c in range(8):
                    P.oi("sync", "dma_start",
                         out=x_fm[xslot][:, c * 512:(c + 1) * 512],
                         in_=d_xbfT[c * 128:(c + 1) * 128,
                                    x_base + t * 512: x_base + t * 512 + 512],
                         _s=sx, _n=16)
                xfm_ready[xslot] = P.val(sx)

            for t in range(NG):
                xslot = t % 2
                xf = x_fm[xslot]
                sx = f"sTPx{xslot}"
                if t == 0 and not (b == 1 and half == 0 and xpre[0]):
                    emit_x_transposes(0)
                if b == 1 and half == 0 and t == 0:
                    xpre[0] = 0
                if t + 1 < NG:
                    emit_x_transposes(t + 1)
                pump_dequant(3)
                P.w("tensor", sx, xfm_ready[xslot])
                if t == 0:
                    P.w("tensor", "sWQ", wq_ready)
                for mi in range(12 if half == 0 else 4):
                    m = half * 12 + mi
                    slot = (half * 48 + t * 12 + mi) % 4
                    ps = PS_B[:, slot * 512:(slot + 1) * 512]
                    sem, val = psBq_free[slot]
                    P.w("tensor", sem, val)
                    for kc in range(8):
                        rec = P.o("tensor", "matmul", ps,
                                  wqh[:, kc * 1536 + mi * 128: kc * 1536 + mi * 128 + 128],
                                  xf[:, kc * 512:(kc + 1) * 512],
                                  start=(kc == 0), stop=(kc == 7))
                    mmv = P.inc(rec, "sT")
                    if mi == 11 or (half == 1 and mi == 3):
                        pass
                    P.w("vector", "sT", mmv)
                    bias = bqkv_c[:, m:m + 1]
                    tcol = t * 512
                    if m < 8:
                        ev = P.oi("vector", "tensor_scalar",
                                  q_fm[:, m * S + tcol: m * S + tcol + 512], ps,
                                  bias, None, op0=ALU.add, _s="sV")
                    else:
                        ev = P.oi("vector", "scalar_tensor_tensor",
                                  k_fm[:, (m - 8) * S + tcol: (m - 8) * S + tcol + 512],
                                  ps, bias, pe2[:, tcol:tcol + 512],
                                  op0=ALU.add, op1=ALU.add, _s="sV")
                    psBq_free[slot] = ("sV", ev)
                if half == 1:
                    for sub in range(4):
                        ktile = t * 4 + sub
                        for vh in range(2):
                            vslot = (t * 8 + sub * 2 + vh) % 4
                            psv = PS_A[:, vslot * 512:(vslot + 1) * 512]
                            sem, val = psA_vfree[vslot]
                            P.w("tensor", sem, val)
                            for kc in range(8):
                                P.o("tensor", "matmul", psv,
                                    xf[:, kc * 512 + sub * 128: kc * 512 + sub * 128 + 128],
                                    wqh[:, kc * 1536 + 512 + vh * 512:
                                         kc * 1536 + 512 + vh * 512 + 512],
                                    start=(kc == 0), stop=False)
                            rec = P.o("tensor", "matmul", psv,
                                      ones_bf[0:65, 0:128],
                                      brow_v[0:65, vh * 512: vh * 512 + 512],
                                      start=False, stop=True)
                            mmv = P.inc(rec, "sT")
                            P.w("vector", "sT", mmv)
                            dst = v_tm[:, ktile * 1280 + vh * 8 * 80:
                                       ktile * 1280 + vh * 8 * 80 + 640]
                            ev = P.oi("vector", "tensor_copy",
                                      dst.rearrange("p (h d) -> p h d", d=80)[:, :, 0:64],
                                      psv.rearrange("p (h d) -> p h d", d=64),
                                      _s="sV")
                            psA_vfree[vslot] = ("sV", ev)
                if half == 1 and t == NG - 1:
                    xfm_cons[xslot] = P.val("sT")
                else:
                    xfm_cons[xslot] = P.val("sT")
        def norm_chunk(buf, c, ncol):
            sqb = sq_sb[ncol % 2]
            P.w("scalar", "sTPx0", sq_guard[0])
            P.w("scalar", "sTPx1", sq_guard[1])
            P.w("scalar", "sT", norm_mm_done[ncol % 2])
            sv = P.oi("scalar", "activation", sqb[:, :], buf[:, c * S:(c + 1) * S],
                      AF.Square, bias=f32(0.0), scale=f32(1.0), _s="sS")
            P.w("tensor", "sS", sv)
            for sl in range(4):
                ps = PS_A[0:1, sl * 512: sl * 512 + 512]
                sem, val = psA_norm_free[sl]
                P.w("tensor", sem, val)
                if sl == 3:
                    mmv = P.oi("tensor", "matmul", ps, ones_bf[:, 0:1],
                               sqb[:, sl * 512:(sl + 1) * 512],
                               start=True, stop=True, _s="sT")
                else:
                    rec = P.o("tensor", "matmul", ps, ones_bf[:, 0:1],
                              sqb[:, sl * 512:(sl + 1) * 512],
                              start=True, stop=True)
                    mmv = P.inc(rec, "sT")
                P.w("vector", "sT", mmv)
                rv = P.oi("vector", "reduce_max", nmax[0:1, 4 * ncol + sl: 4 * ncol + sl + 1],
                          ps, axis=AX.X, _s="sV")
                psA_norm_free[sl] = ("sV", rv)
            norm_mm_done[ncol % 2] = P.val("sT")

        sq_guard = {0: 0, 1: 0}
        norm_mm_done = {0: 0, 1: 0}
        psA_norm_free = list(psA_vfree)
        P.w("scalar", "sV", P.val("sV"))
        ncol = 0
        for buf in (q_fm, k_fm):
            for c in range(8):
                norm_chunk(buf, c, ncol)
                ncol += 1
        psA_free[0] = ("sV", P.val("sV"))
        psA_free[1] = ("sV", P.val("sV"))
        P.d("vector")
        P.o("vector", "reduce_max", nmax[0:1, 64:65], nmax[0:1, 0:64], axis=AX.X)
        P.d("vector")
        cv = P.oi("vector", "tensor_scalar", nmaxbf[0:1, 0:1], nmax[0:1, 64:65],
                  -0.125, None, op0=ALU.mult, _s="sV")
        P.w("tensor", "sV", cv)
        sem, val = psBq_free[0]
        P.w("tensor", sem, val)
        mmv = P.oi("tensor", "matmul", PS_B[:, 0:1], ones_bf[0:1, 0:128],
                   nmaxbf[0:1, 0:1], start=True, stop=True, _s="sT")
        P.w("vector", "sT", mmv)
        cbv = P.oi("vector", "tensor_copy", cneg[:], PS_B[:, 0:1], _s="sV")
        psBq_free[0] = ("sV", cbv)
        pump_dequant(8)
        pfx = "dbg_" if b == 0 else "dbg2_"
        dbg_dump([(pfx + "q", q_fm[:]), (pfx + "k", k_fm[:]), (pfx + "v", v_tm[:]),
                  (pfx + "cneg", cneg[:])])

        P.barrier()

        prev = [None]

        def emit_onorm_tensor():
            if prev[0] is None:
                return
            o_half_p, _, _, rv_p, bmv_hold = prev[0]
            P.w("tensor", "sV", rv_p)
            P.o("tensor", "matmul", o_half_p[64:128, 0:512], ones_f[64:65, 0:64],
                recb[64:65, 0:512], start=True, stop=True, tile_position=(64, 64))
            bmv_hold[0] = P.oi("tensor", "matmul", o_half_p[64:128, 512:1024],
                               ones_f[64:65, 0:64], recb[64:65, 512:1024],
                               start=True, stop=True, tile_position=(64, 64), _s="sT")

        def emit_onorm_vector():
            if prev[0] is None:
                return
            o_half_p, ocol_p, par_p, _, bmv_hold = prev[0]
            prev[0] = None
            P.w("vector", "sT", bmv_hold[0])
            P.o("vector", "tensor_copy", bc_sb[0:64, 0:512], o_half_p[64:128, 0:512])
            P.o("vector", "tensor_copy", bc_sb[64:128, 0:512],
                o_half_p[64:128, 512:1024])
            P.d("vector")
            P.o("vector", "tensor_tensor", O_all[0:64, ocol_p:ocol_p + 512],
                o_half_p[0:64, 0:512], bc_sb[0:64, 0:512], op=ALU.mult)
            ov = P.oi("vector", "tensor_tensor", O_all[64:128, ocol_p:ocol_p + 512],
                      o_half_p[0:64, 512:1024], bc_sb[64:128, 0:512],
                      op=ALU.mult, _s="sV")
            psB_half[par_p] = ("sV", ov)

        for g in range(NG):
            qcol = g * 512
            for p in range(PAIRS):
                par = p % 2
                o_half = PS_B[:, 0:1024] if par == 0 else PS_B[:, 1024:2048]
                exp_vals = []
                pv_done = 0

                def s_pair(kt):
                    s_ps = PS_A[:, 0:1024] if kt % 2 == 0 else PS_A[:, 1024:2048]
                    if kt >= 2:
                        P.w("tensor", "sS", exp_vals[kt - 2][1])
                    elif kt == 0:
                        P.w("tensor", psA_free[0][0], psA_free[0][1])
                        P.w("tensor", psA_free[1][0], psA_free[1][1])
                    rec = P.o("tensor", "matmul", s_ps[:, 0:512],
                              k_fm[0:64, p * S + kt * 128: p * S + kt * 128 + 128],
                              q_fm[0:64, p * S + qcol: p * S + qcol + 512],
                              start=True, stop=True, tile_position=(0, 0))
                    sv1 = P.inc(rec, "sT")
                    rec = P.o("tensor", "matmul", s_ps[:, 512:1024],
                              k_fm[64:128, p * S + kt * 128: p * S + kt * 128 + 128],
                              q_fm[64:128, p * S + qcol: p * S + qcol + 512],
                              start=True, stop=True, tile_position=(64, 0))
                    sv2 = P.inc(rec, "sT")
                    P.w("scalar", "sT", sv1)
                    ea = P.oi("scalar", "activation", pT[kt % 2][:, 0:512],
                              s_ps[:, 0:512], AF.Exp, bias=cneg[:, :],
                              scale=f32(0.125), _s="sS")
                    P.w("scalar", "sT", sv2)
                    eb = P.oi("scalar", "activation", pT[kt % 2][:, 512:1024],
                              s_ps[:, 512:1024], AF.Exp, bias=cneg[:, :],
                              scale=f32(0.125), _s="sS")
                    exp_vals.append((ea, eb))

                s_pair(0)
                for kt in range(KT):
                    if kt + 1 < KT:
                        s_pair(kt + 1)
                    if kt == 6:
                        emit_onorm_tensor()
                    P.w("tensor", "sS", exp_vals[kt][0])
                    if kt == 0:
                        P.w("tensor", psB_half[par][0], psB_half[par][1])
                    P.o("tensor", "matmul", o_half[0:65, 0:512],
                        v_tm[:, kt * 1280 + 2 * p * 80: kt * 1280 + 2 * p * 80 + 65],
                        pT[kt % 2][:, 0:512], start=(kt == 0), stop=(kt == KT - 1))
                    P.w("tensor", "sS", exp_vals[kt][1])
                    rec = P.o("tensor", "matmul", o_half[0:65, 512:1024],
                              v_tm[:, kt * 1280 + (2 * p + 1) * 80:
                                   kt * 1280 + (2 * p + 1) * 80 + 65],
                              pT[kt % 2][:, 512:1024],
                              start=(kt == 0), stop=(kt == KT - 1))
                    if kt == KT - 1:
                        pv_done = P.inc(rec, "sT")
                psA_free[0] = ("sS", exp_vals[KT - 2][1])
                psA_free[1] = ("sS", exp_vals[KT - 1][1])
                emit_onorm_vector()
                P.w("vector", "sT", pv_done)
                P.o("vector", "reciprocal", recb[64:65, 0:512], o_half[64:65, 0:512])
                rv = P.oi("vector", "reciprocal", recb[64:65, 512:1024],
                          o_half[64:65, 512:1024], _s="sV")
                prev[0] = (o_half, p * S + qcol, par, rv, [0])
                pump_dequant(3)
        emit_onorm_tensor()
        emit_onorm_vector()
        dbg_dump([(("dbg_" if b == 0 else "dbg2_") + "O", O_all[:])])

        P.barrier()

        fc2_banks = [PS_A[:, 0:1024], PS_A[:, 1024:2048],
                     PS_B[:, 0:1024], PS_B[:, 1024:2048]]
        fc2_free = [("sV", 0)] * 4
        psBq_free = [psB_half[0], psB_half[0], psB_half[1], psB_half[1]]
        fc1_mm = {}
        uv_j = {}
        hv_j = {}
        chain_agv = {}
        chain_ag2 = {}
        oall_done = [0]
        hpe_last = [0]
        t2_war = [0]

        w1ring = w1buf + [w1buf_x]
        w2ring = w2buf + [w2buf_x]

        def xt_prefetch(g2, j):
            row = x_base + g2 * 512 + j * 128
            P.w("sync", "sV", xt_cons[j % 2])
            xt_loads[j % 2] = P.oi("sync", "dma_start", out=x_tm2[j % 2][:],
                                   in_=x_rows[row:row + 128, :],
                                   _s=f"sXT{j % 2}", _n=16)

        def partA_tv(g2, j):
            ps_u = PS_A[:, (j % 2) * 1024:(j % 2) * 1024 + 1024]
            x1j = x1buf[g2 % 2][:, j * DM:(j + 1) * DM]
            P.w("tensor", psA_free[j % 2][0], psA_free[j % 2][1])
            for hf in range(2):
                for p in range(8):
                    P.o("tensor", "matmul", ps_u[:, hf * 512:(hf + 1) * 512],
                        O_all[:, p * S + g2 * 512 + j * 128:
                              p * S + g2 * 512 + j * 128 + 128],
                        wout_bf[:, p * 1024 + hf * 512: p * 1024 + (hf + 1) * 512],
                        start=(p == 0), stop=False)
                rec = P.o("tensor", "matmul", ps_u[:, hf * 512:(hf + 1) * 512],
                          ones_bf[0:65, 0:128],
                          brow_o[0:65, hf * 512: hf * 512 + 512],
                          start=False, stop=True)
            opv = P.inc(rec, "sT")
            if g2 == 3 and j == 3:
                oall_done[0] = opv
            P.w("vector", "sT", opv)
            P.w("vector", f"sXT{j % 2}", xt_loads[j % 2])
            uv = P.oi("vector", "tensor_tensor", x1j, ps_u, x_tm2[j % 2][:],
                      op=ALU.add, _s="sV")
            uv_j[j] = uv
            psA_free[j % 2] = ("sV", uv)
            xt_cons[j % 2] = uv
            if j + 2 < 4:
                xt_prefetch(g2, j + 2)
            P.d("vector")
            P.o("vector", "bn_stats", stats[:, 0:6], x1j[:, 0:512])
            P.o("vector", "bn_stats", stats[:, 6:12], x1j[:, 512:1024])
            P.d("vector")
            chain_agv[j] = P.oi("vector", "bn_aggr", mvp[:, 3 * j:3 * j + 2],
                                stats[:, 0:12], _s="sV")

        def batch_scalarA():
            P.w("scalar", "sV", chain_agv[3])
            for j in range(4):
                P.o("scalar", "activation", mvp[:, 3 * j + 2:3 * j + 3],
                    mvp[:, 3 * j + 1:3 * j + 2], AF.Ln, bias=f32(EPS), scale=f32(1.0))
                P.o("scalar", "activation", mvp[:, 12 + 3 * j + 2:12 + 3 * j + 3],
                    mvp[:, 3 * j + 1:3 * j + 2], AF.Ln, bias=f32(EPS * EPS),
                    scale=f32(1.0 + EPS))
            P.d("scalar")
            for j in range(4):
                P.o("scalar", "activation", mvp[:, 3 * j + 2:3 * j + 3],
                    mvp[:, 3 * j + 2:3 * j + 3], AF.Exp, bias=f32(0.0),
                    scale=f32(-0.5))
                rec = P.o("scalar", "activation", mvp[:, 12 + 3 * j + 2:12 + 3 * j + 3],
                          mvp[:, 12 + 3 * j + 2:12 + 3 * j + 3], AF.Exp, bias=f32(0.0),
                          scale=f32(-0.5))
            mvp_sc[0] = P.inc(rec, "sS")

        def applyAB(g2, j):
            x1j = x1buf[g2 % 2][:, j * DM:(j + 1) * DM]
            c0 = 3 * j
            c1 = 12 + 3 * j
            P.w("vector", "sS", mvp_sc[0])
            P.o("vector", "tensor_scalar", t2_sb[:], x1j, mvp[:, c0:c0 + 1],
                mvp[:, c1 + 2:c1 + 3], op0=ALU.subtract, op1=ALU.mult)
            P.d("vector")
            P.o("vector", "tensor_scalar", x1j, x1j, mvp[:, c0:c0 + 1],
                mvp[:, c0 + 2:c0 + 3], op0=ALU.subtract, op1=ALU.mult)
            P.o("vector", "tensor_tensor", u_sb[:], t2_sb[:], gam_bc[:], op=ALU.mult)
            P.d("vector")
            P.w("vector", htm_cons[j][0], htm_cons[j][1])
            hv_j[j] = P.oi("vector", "tensor_tensor", h_tm[j][:], u_sb[:], bet_bc[:],
                           op=ALU.add, _s="sV")

        def h_transposes_pe(j):
            half = PS_A[:, (j % 2) * 1024:(j % 2) * 1024 + 1024]
            P.w("tensor", "sV", hv_j[j])
            P.w("tensor", psA_free[j % 2][0], psA_free[j % 2][1])
            for c in range(8):
                rec = P.o("tensor", "matmul", half[:, c * 128:(c + 1) * 128],
                          h_tm[j][:, c * 128:(c + 1) * 128], eye_sb[:],
                          start=True, stop=True)
            mmv = P.inc(rec, "sT")
            htm_cons[j] = ("sT", mmv)
            P.w("scalar", "sT", mmv)
            for c in range(8):
                rec2 = P.o("scalar", "activation",
                           h_fm[:, c * 512 + j * 128: c * 512 + j * 128 + 128],
                           half[:, c * 128:(c + 1) * 128], AF.Copy,
                           bias=0.0, scale=1.0)
            ev = P.inc(rec2, "sS")
            psA_free[j % 2] = ("sS", ev)
            hpe_last[0] = ev

        def h_transposes(g2, j, also_wait_fc1=0):
            P.w("sync", "sV", hv_j[j])
            if also_wait_fc1:
                P.w("sync", "sT", also_wait_fc1)
            for c in range(8):
                P.oi("sync", "dma_start",
                     out=h_fm[:, c * 512 + j * 128: c * 512 + j * 128 + 128],
                     in_=h_tm[j][:, c * 128:(c + 1) * 128], transpose=True,
                     _s=f"sTPh{j}", _n=16)
            htm_cons[j] = (f"sTPh{j}", P.val(f"sTPh{j}"))

        partA_tv_at = {3: 0, 7: 1, 11: 2, 15: 3}
        for g in range(NG):
            if g == 0:
                xt_prefetch(0, 0)
                xt_prefetch(0, 1)
                for j in range(4):
                    partA_tv(0, j)
                batch_scalarA()
                for j in range(4):
                    applyAB(0, j)
                    h_transposes_pe(j)
            if g == 0:
                P.w("tensor", "sS", hpe_last[0])
            else:
                for jj in range(4):
                    P.w("tensor", f"sTPh{jj}", P.val(f"sTPh{jj}"))
            if b == 0 and g == 0:
                pump_dequant(DQ_TOTAL)
                assert dq_i[0] == DQ_TOTAL
                P.w("gpsimd", "sSC0", sc_slot[0])
                P.w("gpsimd", "sSC1", sc_slot[1])
            silu_vals = {}
            for f in range(32):
                wb3 = f % 3
                sw = "sW1" + "abc"[wb3]
                P.w("gpsimd", "sT", w1_cons[wb3])
                wv1 = P.oi("gpsimd", "dma_start",
                           out=w1ring[wb3][:].rearrange("p (k m) -> p k m", k=8),
                           in_=d_w1s[:, f * 128:(f + 1) * 128].rearrange(
                               "(k p) m -> p k m", p=128), _s=sw, _n=16)
                P.w("tensor", sw, wv1)
                ps = PS_B[:, (f % 4) * 512:(f % 4) * 512 + 512]
                if f >= 4:
                    P.w("tensor", "sS", silu_vals[f - 4])
                else:
                    P.w("tensor", psBq_free[f % 4][0], psBq_free[f % 4][1])
                for kc in range(8):
                    rec = P.o("tensor", "matmul", ps,
                              w1ring[wb3][:, kc * 128:(kc + 1) * 128],
                              h_fm[:, kc * 512:(kc + 1) * 512],
                              start=(kc == 0), stop=(kc == 7))
                fc1_mm[f] = w1_cons[wb3] = P.inc(rec, "sT")
                P.w("scalar", "sT", fc1_mm[f])
                silu_vals[f] = P.oi("scalar", "activation",
                                    y_g[:, f * 512:(f + 1) * 512], ps, AF.Silu,
                                    bias=b1_c[:, f:f + 1], scale=f32(1.0), _s="sS")
                if g + 1 < NG:
                    if f == 1:
                        xt_prefetch(g + 1, 0)
                        xt_prefetch(g + 1, 1)
                    if f in partA_tv_at:
                        partA_tv(g + 1, partA_tv_at[f])
                    if f == 17:
                        batch_scalarA()
                    if f in (19, 22, 25, 28):
                        applyAB(g + 1, (f - 19) // 3)
            for sl in range(4):
                psBq_free[sl] = ("sS", silu_vals[28 + sl])
            for f in range(32):
                wb3 = f % 3
                sw = "sW2" + "abc"[wb3]
                P.w("gpsimd", "sT", w2_cons[wb3])
                wv2 = P.oi("gpsimd", "dma_start", out=w2ring[wb3][:],
                           in_=d_w2s[f * 128:(f + 1) * 128, :], _s=sw, _n=16)
                P.w("tensor", sw, wv2)
                if f == 0:
                    P.w("tensor", "sS", silu_vals[31])
                    if g + 1 < NG:
                        P.w("tensor", "sV", uv_j[2])
                        P.w("tensor", "sV", uv_j[3])
                    for jj in range(4):
                        P.w("tensor", fc2_free[jj][0], fc2_free[jj][1])
                for j in range(4):
                    for hf in range(2):
                        rec = P.o("tensor", "matmul",
                                  fc2_banks[j][:, hf * 512:(hf + 1) * 512],
                                  y_g[:, f * 512 + j * 128: f * 512 + j * 128 + 128],
                                  w2ring[wb3][:, hf * 512:(hf + 1) * 512],
                                  start=(f == 0), stop=False)
                w2_cons[wb3] = P.inc(rec, "sT")
                if g + 1 < NG and f in (1, 4, 7, 10):
                    h_transposes(g + 1, (f - 1) // 3,
                                 also_wait_fc1=fc1_mm[31] if f == 1 else 0)
            for j in range(4):
                for hf in range(2):
                    rec = P.o("tensor", "matmul",
                              fc2_banks[j][:, hf * 512:(hf + 1) * 512],
                              ones_bf[0:65, 0:128],
                              brow_2[0:65, hf * 512: hf * 512 + 512],
                              start=False, stop=True)
            z_done = P.inc(rec, "sT")
            for j in (2, 3, 0, 1):
                ob = j % 2
                P.w("vector", "sT", z_done)
                P.w("vector", f"sOUT{ob}", out_cons[ob])
                x1j = x1buf[g % 2][:, j * DM:(j + 1) * DM]
                P.w("vector", "sV", t2_war[0])
                tv = P.oi("vector", "tensor_tensor", t2_sb[:], fc2_banks[j], x1j,
                          op=ALU.add, _s="sV")
                fc2_free[j] = ("sV", tv)
                P.d("vector")
                P.o("vector", "bn_stats", stats3[:, 0:6], t2_sb[:, 0:512])
                P.o("vector", "bn_stats", stats3[:, 6:12], t2_sb[:, 512:1024])
                P.d("vector")
                P.w("vector", "sS", mvp_sc[2])
                agv = P.oi("vector", "bn_aggr", mvp2[:, 0:2], stats3[:, 0:12], _s="sV")
                P.w("scalar", "sV", agv)
                P.o("scalar", "activation", mvp2[:, 2:3], mvp2[:, 1:2], AF.Ln,
                    bias=f32(EPS), scale=f32(1.0))
                P.d("scalar")
                mvp_sc[2] = P.oi("scalar", "activation", mvp2[:, 2:3], mvp2[:, 2:3],
                                 AF.Exp, bias=f32(0.0), scale=f32(-0.5), _s="sS")
                P.w("vector", "sS", mvp_sc[2])
                wv = P.oi("vector", "tensor_scalar", out_sb[ob][:], t2_sb[:],
                          mvp2[:, 0:1], mvp2[:, 2:3], op0=ALU.subtract, op1=ALU.mult,
                          _s="sV")
                t2_war[0] = wv
                P.w("sync", "sV", wv)
                row = x_base + g * 512 + j * 128
                out_cons[ob] = P.oi("sync", "dma_start", out=y_rows[row:row + 128, :],
                                    in_=out_sb[ob][:], _s=f"sOUT{ob}", _n=16)
            psA_free[0] = fc2_free[0]
            psA_free[1] = fc2_free[1]
            psBq_free[0] = psBq_free[1] = fc2_free[2]
            psBq_free[2] = psBq_free[3] = fc2_free[3]
            if g == 0:
                pfx2 = "dbg_" if b == 0 else "dbg2_"
                dbg_dump([(pfx2 + "x1", x1buf[0][:]), (pfx2 + "h", h_fm[:]),
                          (pfx2 + "y", y_g[:])])
        psB_half = {0: fc2_free[2], 1: fc2_free[3]}
        if b == 0:
            P.w("gpsimd", "sT", oall_done[0])
            for kc in range(8):
                P.oi("gpsimd", "dma_start",
                     out=wqh[:, kc * 1536:(kc + 1) * 1536],
                     in_=d_wqkv[kc * 128:(kc + 1) * 128, 0:1536], _s="sWQ", _n=16)
            wq_pre[0] = P.val("sWQ")
            P.w("sync", "sT", max(oall_done[0], xfm_cons[0]))
            P.w("sync", "sXC1", 16)
            for c in range(8):
                P.oi("sync", "dma_start",
                     out=x_fm[0][:, c * 512:(c + 1) * 512],
                     in_=d_xbfT[c * 128:(c + 1) * 128, S: S + 512],
                     _s="sTPx0", _n=16)
            xfm_ready[0] = P.val("sTPx0")
            xpre[0] = 1
        P.barrier()

    assert dq_i[0] == DQ_TOTAL
    assert debug or P.val("sOUT0") + P.val("sOUT1") == 16 * NB * 16
    P.w("sync", "sOUT0", P.val("sOUT0"))
    P.w("sync", "sOUT1", P.val("sOUT1"))

    def run_ops(e, eng_name):
        for rec in P.ops[eng_name]:
            if rec[0] == "w":
                e.wait_ge(sems[rec[1]], rec[2])
            elif rec[0] == "d":
                e.drain()
            else:
                ins = getattr(e, rec[1])(*rec[2], **rec[3])
                if rec[4] is not None:
                    ins.then_inc(sems[rec[4][0]], rec[4][1])

    with es, nc.Block() as block:
        @block.sync
        def _(e):
            run_ops(e, "sync")

        @block.gpsimd
        def _(e):
            run_ops(e, "gpsimd")

        @block.tensor
        def _(e):
            run_ops(e, "tensor")

        @block.vector
        def _(e):
            run_ops(e, "vector")

        @block.scalar
        def _(e):
            run_ops(e, "scalar")

    return nc


def _pos_enc():
    positions = np.arange(S, dtype=np.float32)[:, None]
    div_term = np.exp(np.arange(0, 64, 2, dtype=np.float32)
                      * f32(-(np.log(10000.0) / 64)))
    pe = np.zeros((S, 64), dtype=np.float32)
    pe[:, 0::2] = np.sin(positions * div_term)
    pe[:, 1::2] = np.cos(positions * div_term)
    return pe


_return_in_maps = [False]


def kernel(L0_attn_x, L0_attn_W_qkv, L0_attn_b_qkv, L0_attn_W_out, L0_attn_b_out,
           L0_ffn_ln_gamma, L0_ffn_ln_beta, L0_ffn_fc1_q, L0_ffn_fc1_scales,
           L0_ffn_fc1_bias, L0_ffn_fc2_q, L0_ffn_fc2_scales, L0_ffn_fc2_bias):
    x = np.ascontiguousarray(np.asarray(L0_attn_x, dtype=np.float32))
    wqkv = np.ascontiguousarray(np.asarray(L0_attn_W_qkv, dtype=np.float32))
    bqkv = np.asarray(L0_attn_b_qkv, dtype=np.float32)
    wout = np.ascontiguousarray(np.asarray(L0_attn_W_out, dtype=np.float32))
    bout = np.asarray(L0_attn_b_out, dtype=np.float32)
    gam = np.asarray(L0_ffn_ln_gamma, dtype=np.float32)
    bet = np.asarray(L0_ffn_ln_beta, dtype=np.float32)
    q1 = np.ascontiguousarray(np.asarray(L0_ffn_fc1_q, dtype=np.int32))
    s1 = np.asarray(L0_ffn_fc1_scales, dtype=np.float32)
    b1 = np.asarray(L0_ffn_fc1_bias, dtype=np.float32)
    q2 = np.ascontiguousarray(np.asarray(L0_ffn_fc2_q, dtype=np.int32))
    s2 = np.asarray(L0_ffn_fc2_scales, dtype=np.float32)
    b2 = np.asarray(L0_ffn_fc2_bias, dtype=np.float32)

    pe = _pos_enc()
    shared = dict(
        wqkv=wqkv,
        wout=wout,
        bqkv_c=np.ascontiguousarray(bqkv.reshape(24, 128).T),
        bout_r=np.ascontiguousarray(bout.reshape(1, DM)),
        bv_r=np.ascontiguousarray(bqkv[2048:3072].reshape(1, DM)),
        gam_bc=np.ascontiguousarray(np.broadcast_to(gam[None, :], (128, DM))),
        bet_bc=np.ascontiguousarray(np.broadcast_to(bet[None, :], (128, DM))),
        q1=q1,
        s1bc=np.ascontiguousarray(
            np.broadcast_to(s1[:, None, :], (8, 128, FF)).reshape(DM, FF)),
        b1_c=np.ascontiguousarray(b1.reshape(32, 128).T),
        q2=q2,
        s2bc=np.ascontiguousarray(
            np.broadcast_to(s2[:, None, :], (32, 128, DM)).reshape(FF, DM)),
        b2_r=np.ascontiguousarray(b2.reshape(1, DM)),
        pe2=np.ascontiguousarray(np.concatenate([pe.T, pe.T], axis=0)),
    )
    in_maps = []
    for c in range(8):
        m = dict(shared)
        m["x"] = np.ascontiguousarray(x[c * NB:(c + 1) * NB])
        m["xt"] = np.ascontiguousarray(
            x[c * NB:(c + 1) * NB].reshape(NB * 2048, 1024).T)
        m["eye"] = np.eye(128, dtype=np.float32).astype(ml_dtypes.bfloat16)
        in_maps.append(m)
    if _return_in_maps[0]:
        return in_maps

    nc = build_program()
    res = run_bass_kernel_spmd(nc, in_maps, core_ids=list(range(8)))
    out = np.concatenate([np.asarray(r["y"], dtype=np.float32) for r in res.results],
                         axis=0)
    return out
